# version 03
# baseline (speedup 1.0000x reference)
import numpy as np
import ml_dtypes
import concourse.bass as bass
import concourse.mybir as mybir
from concourse.bass_utils import run_bass_kernel_spmd

F32 = mybir.dt.float32
BF16 = mybir.dt.bfloat16
AF = mybir.ActivationFunctionType
ALU = mybir.AluOpType
AX = mybir.AxisListType

D_MODEL = 4096
D_INNER = 8192
NH = 128
HD = 64
NG = 8
DS = 128
OC = 18560
Z_OC = 8192
XBC_OC = 10240
TOK = 1024
CH = 128
NCH = TOK // CH
KT = D_MODEL // 128
EPS = 1e-5
ALL_PHASES = ("p1", "p19", "p15", "p2", "p29", "p2c", "p3", "p4", "p5")


class Ctr:
    def __init__(self):
        self.v = {}

    def add(self, name, n):
        self.v[name] = self.v.get(name, 0) + n
        return self.v[name]

    def get(self, name):
        return self.v.get(name, 0)


class Chain:

    def __init__(self, nc, ctx, name, n=2):
        self.sems = [ctx.enter_context(nc.semaphore(f"{name}{i}")) for i in range(n)]
        self.n = n
        self.count = [0] * n
        self.marks = []

    def issue(self, eng, out, in_):
        j = len(self.marks) % self.n
        ins = eng.dma_start(out, in_).then_inc(self.sems[j], 16)
        self.count[j] += 1
        self.marks.append((j, 16 * self.count[j]))
        return ins

    def wait(self, eng, idx):
        j = idx % self.n
        eng.wait_ge(self.sems[j], 16 * (idx // self.n + 1))

    def wait_last(self, eng):
        for j in range(self.n):
            if self.count[j]:
                eng.wait_ge(self.sems[j], 16 * self.count[j])


def build_program(debug=(), phases=ALL_PHASES):
    nc = bass.Bass("TRN2", target_bir_lowering=False)

    d_xT = nc.dram_tensor("xT", [128, KT * TOK], F32, kind="ExternalInput")
    d_xres = nc.dram_tensor("xres", [TOK, D_MODEL], F32, kind="ExternalInput")
    d_wip = nc.dram_tensor("wipT", [D_MODEL, OC], F32, kind="ExternalInput")
    d_wo = nc.dram_tensor("woT", [D_INNER, D_MODEL], F32, kind="ExternalInput")
    d_wb = nc.dram_tensor("wbT", [D_MODEL, D_MODEL], F32, kind="ExternalInput")
    d_wg = nc.dram_tensor("wgT", [128, KT * 8], F32, kind="ExternalInput")
    d_aT = nc.dram_tensor("aT", [128, KT * 128], F32, kind="ExternalInput")
    d_bTf = nc.dram_tensor("bTf", [128, D_MODEL], F32, kind="ExternalInput")
    d_convw = nc.dram_tensor("convw", [128, 80 * 4], F32, kind="ExternalInput")
    d_convb = nc.dram_tensor("convb", [128, 80], F32, kind="ExternalInput")
    d_dtb = nc.dram_tensor("dtb", [128, 1], F32, kind="ExternalInput")
    d_alog = nc.dram_tensor("alog", [128, 1], F32, kind="ExternalInput")
    d_drow = nc.dram_tensor("drow", [1, 128], F32, kind="ExternalInput")
    d_nrmw = nc.dram_tensor("nrmw", [128, 64], F32, kind="ExternalInput")
    d_bbase = nc.dram_tensor("bbase", [1, D_MODEL], F32, kind="ExternalInput")
    d_mask = nc.dram_tensor("maskut", [128, 128], F32, kind="ExternalInput")
    d_id = nc.dram_tensor("ident", [128, 128], F32, kind="ExternalInput")
    d_role = nc.dram_tensor("role", [128, 2], F32, kind="ExternalInput")

    d_out = nc.dram_tensor("out", [TOK, D_MODEL], F32, kind="ExternalOutput")

    z_sp = nc.dram_tensor("z_sp", [TOK, D_INNER], BF16)
    xr_sp = nc.dram_tensor("xr_sp", [XBC_OC, TOK], BF16)
    xc_sp = nc.dram_tensor("xc_sp", [XBC_OC, TOK], BF16)
    yr_sp = nc.dram_tensor("yr_sp", [TOK, D_INNER], BF16)
    yT_sp = nc.dram_tensor("yT_sp", [D_INNER, TOK], BF16)
    x1_sp = nc.dram_tensor("x1_sp", [TOK, D_MODEL], BF16)
    x2_sp = nc.dram_tensor("x2_sp", [TOK, D_MODEL], BF16)
    h_in = nc.dram_tensor("h_in", [128, 240], BF16)
    h_out = nc.dram_tensor("h_out", [128, 240], BF16)
    st_in = nc.dram_tensor("st_in", [128, D_INNER], BF16)
    st_out = nc.dram_tensor("st_out", [128, D_INNER], BF16)

    dbg = {}
    if "p1" in debug:
        dbg["z"] = nc.dram_tensor("dbg_z", [TOK, D_INNER], BF16, kind="ExternalOutput")
        dbg["xr"] = nc.dram_tensor("dbg_xr", [XBC_OC, TOK], BF16, kind="ExternalOutput")
        dbg["xc"] = nc.dram_tensor("dbg_xc", [XBC_OC, TOK], BF16, kind="ExternalOutput")
        dbg["dt"] = nc.dram_tensor("dbg_dt", [128, TOK], F32, kind="ExternalOutput")
    if "p2" in debug:
        dbg["yr"] = nc.dram_tensor("dbg_yr", [TOK, D_INNER], BF16, kind="ExternalOutput")
        dbg["st"] = nc.dram_tensor("dbg_st", [128, D_INNER], BF16, kind="ExternalOutput")
    if "p2c" in debug:
        dbg["yT"] = nc.dram_tensor("dbg_yT", [D_INNER, TOK], BF16, kind="ExternalOutput")
    if "p3" in debug:
        dbg["x1"] = nc.dram_tensor("dbg_x1", [TOK, D_MODEL], BF16, kind="ExternalOutput")
    if "p4" in debug:
        dbg["x2"] = nc.dram_tensor("dbg_x2", [TOK, D_MODEL], BF16, kind="ExternalOutput")
        dbg["x1T"] = nc.dram_tensor("dbg_x1T", [128, KT * TOK], BF16, kind="ExternalOutput")
        dbg["rt"] = nc.dram_tensor("dbg_rt", [128, 64], BF16, kind="ExternalOutput")
        dbg["hwT"] = nc.dram_tensor("dbg_hwT", [128, TOK], BF16, kind="ExternalOutput")

    C = Ctr()

    class _StopBuild(Exception):
        pass

    import contextlib
    ctx = contextlib.ExitStack()
    open_stacks = []

    def _mkstack(cl):
        st = cl.ExitStack()
        open_stacks.append(st)
        return st

    def _close(st):
        assert open_stacks and open_stacks[-1] is st
        open_stacks.pop()
        st.close()

    with ctx:
      try:
        sMM = ctx.enter_context(nc.semaphore("sMM"))
        sEV = ctx.enter_context(nc.semaphore("sEV"))
        sDV = ctx.enter_context(nc.semaphore("sDV"))
        sCC = ctx.enter_context(nc.semaphore("sCC"))
        sVS = ctx.enter_context(nc.semaphore("sVS"))
        vs_n = [0]

        def vedge(v, ins):
            vs_n[0] += 1
            ins.then_inc(sVS, 1)
            v.wait_ge(sVS, vs_n[0])
        cG0 = Chain(nc, ctx, "cG0", 2)
        cW = Chain(nc, ctx, "cW", 2)
        cSPz = Chain(nc, ctx, "cSPz", 2)
        cH0 = Chain(nc, ctx, "cH0", 2)
        cCV = Chain(nc, ctx, "cCV", 2)
        cCS = Chain(nc, ctx, "cCS", 2)
        cXC = Chain(nc, ctx, "cXC", 2)

        sb_convw = ctx.enter_context(nc.sbuf_tensor("c_convw", [128, 320], F32))
        sb_convb = ctx.enter_context(nc.sbuf_tensor("c_convb", [128, 80], F32))
        sb_dtb = ctx.enter_context(nc.sbuf_tensor("c_dtb", [128, 1], F32))
        sb_A = ctx.enter_context(nc.sbuf_tensor("c_A", [128, 1], F32))
        sb_nrmw = ctx.enter_context(nc.sbuf_tensor("c_nrmw", [128, 64], F32))
        sb_maskb = ctx.enter_context(nc.sbuf_tensor("c_maskb", [128, 128], BF16))
        sb_identb = ctx.enter_context(nc.sbuf_tensor("c_identb", [128, 128], BF16))
        sb_ones1 = ctx.enter_context(nc.sbuf_tensor("c_ones1", [1, 128], BF16))
        sb_role = ctx.enter_context(nc.sbuf_tensor("c_role", [128, 2], F32))
        sb_Db = ctx.enter_context(nc.sbuf_tensor("c_Db", [128, 128], BF16))
        sb_drow = ctx.enter_context(nc.sbuf_tensor("c_drow", [1, 128], BF16))
        sb_halo = ctx.enter_context(nc.sbuf_tensor("c_halo", [128, 240], BF16))
        sb_halou = ctx.enter_context(nc.sbuf_tensor("c_halou", [128, 240], BF16))
        sb_dtraw = ctx.enter_context(nc.sbuf_tensor("c_dtraw", [128, TOK], F32))
        sb_dt = ctx.enter_context(nc.sbuf_tensor("c_dt", [128, TOK], F32))
        sb_cs = ctx.enter_context(nc.sbuf_tensor("c_cs", [128, TOK], F32))
        sb_rs = ctx.enter_context(nc.sbuf_tensor("c_rs", [128, NCH], F32))
        sb_sig = ctx.enter_context(nc.sbuf_tensor("c_sig", [128, NCH + 1], F32))
        sb_mean = ctx.enter_context(nc.sbuf_tensor("c_mean", [128, 8], F32))
        sb_rstd = ctx.enter_context(nc.sbuf_tensor("c_rstd", [128, 8], F32))
        sb_mean2 = ctx.enter_context(nc.sbuf_tensor("c_mean2", [128, 8], F32))
        sb_eps = ctx.enter_context(nc.sbuf_tensor("c_eps", [128, 1], F32))
        S_sb = ctx.enter_context(nc.sbuf_tensor("p2_S", [128, D_INNER], BF16))
        sb_rstd2 = ctx.enter_context(nc.sbuf_tensor("c_rstd2", [128, 8], F32))

        import contextlib as _cl
        p1_ctx = _mkstack(_cl)
        ps = [p1_ctx.enter_context(nc.psum_tensor(f"ps{i}", [128, 512], F32)) for i in range(8)]
        with nc.Block() as blk:
            @blk.sync
            def _(s):
                cH0.issue(s, sb_convw[:, :], d_convw[:, :])
                cH0.issue(s, sb_convb[:, :], d_convb[:, :])
                cH0.issue(s, sb_dtb[:, :], d_dtb[:, :])
                cH0.issue(s, sb_A[:, :], d_alog[:, :])
                cH0.issue(s, sb_nrmw[:, :], d_nrmw[:, :])
                cH0.issue(s, sb_role[:, :], d_role[:, :])

            @blk.gpsimd
            def _(g):
                cG0.issue(g, sb_maskb[:, :], d_mask[:, :])
                cG0.issue(g, sb_identb[:, :], d_id[:, :])
                cG0.issue(g, sb_drow[:, :], d_drow[:, :])

            @blk.vector
            def _(v):
                cG0.wait_last(v)
                cH0.wait_last(v)
                v.memset(sb_eps[:, :], EPS)
                v.memset(sb_ones1[:, :], 1.0).then_inc(sDV, 1)

            @blk.scalar
            def _(sc):
                cH0.wait_last(sc)
                sc.activation(sb_cs[:, 0:1], sb_A[:, :], AF.Exp).then_inc(sEV, 1)

            @blk.tensor
            def _(t):
                cG0.wait_last(t)
                t.wait_ge(sDV, C.get(sDV) + 1)
                t.matmul(ps[7][:, 0:128], sb_ones1[:, :], sb_drow[:, :],
                         start=True, stop=True).then_inc(sMM, 1)
        C.add(sEV, 1)
        C.add(sMM, 1)
        C.add(sDV, 1)

        with nc.Block() as blk:
            @blk.vector
            def _(v):
                v.tensor_scalar_mul(sb_A[:, :], sb_cs[:, 0:1], -1.0)
                v.tensor_copy(sb_Db[:, :], ps[7][:, 0:128])

        if "p1" not in phases:
            raise _StopBuild()

        chunks = [("z", i) for i in range(16)] + [("x", i) for i in range(20)] + [("dt", 0)]
        nchunks = len(chunks)
        banks_per_chunk = [8] * 36 + [2]

        def cum_banks(i):
            return sum(banks_per_chunk[:i])

        sb_xTb = p1_ctx.enter_context(nc.sbuf_tensor("p1_xTb", [128, KT * TOK], BF16))
        wbufs = [p1_ctx.enter_context(nc.sbuf_tensor(f"p1_w{i}", [128, KT * 512], BF16)) for i in range(2)]
        zstg = [p1_ctx.enter_context(nc.sbuf_tensor(f"p1_zs{i}", [128, 8 * 512], BF16)) for i in range(2)]

        spill_idx = {}
        _nsp = 0
        for i, (kind, idx) in enumerate(chunks):
            if kind != "dt":
                spill_idx[i] = _nsp
                _nsp += 1

        with nc.Block() as blk:
            @blk.gpsimd
            def _(g):
                xt_issue = len(cG0.marks)
                cG0.issue(g, sb_xTb[:, :], d_xT[:, :])
                for i, (kind, idx) in enumerate(chunks):
                    if kind == "z":
                        c0, w = idx * 512, 512
                    elif kind == "x":
                        c0, w = Z_OC + idx * 512, 512
                    else:
                        c0, w = Z_OC + XBC_OC, 128
                    if i >= 2:
                        g.wait_ge(sMM, C.get(sMM) + cum_banks(i - 1))
                    cW.issue(g,
                             wbufs[i % 2][:, 0:KT * w].rearrange("p (k c) -> p k c", k=KT),
                             d_wip[:, c0:c0 + w].rearrange("(k p) c -> p k c", p=128))

            @blk.tensor
            def _(t):
                cG0.wait(t, 3)
                for i, (kind, idx) in enumerate(chunks):
                    cW.wait(t, i)
                    wb = wbufs[i % 2][:, :].rearrange("p (k c) -> p k c", k=KT)
                    if kind == "z":
                        for tt in range(8):
                            if i >= 1:
                                t.wait_ge(sEV, C.get(sEV) + cum_banks(i - 1) + tt + 1)
                            for k in range(KT):
                                mi = t.matmul(ps[tt][:, :],
                                              sb_xTb[:, k * TOK + tt * 128: k * TOK + (tt + 1) * 128],
                                              wb[:, k, 0:512],
                                              start=(k == 0), stop=(k == KT - 1))
                            mi.then_inc(sMM, 1)
                    elif kind == "x":
                        for oct_ in range(4):
                            for t2 in range(2):
                                bi = oct_ * 2 + t2
                                if i >= 1:
                                    t.wait_ge(sEV, C.get(sEV) + cum_banks(i - 1) + bi + 1)
                                for k in range(KT):
                                    mi = t.matmul(ps[bi][:, :],
                                                  wb[:, k, oct_ * 128:(oct_ + 1) * 128],
                                                  sb_xTb[:, k * TOK + t2 * 512: k * TOK + (t2 + 1) * 512],
                                                  start=(k == 0), stop=(k == KT - 1))
                                mi.then_inc(sMM, 1)
                    else:
                        wbd = wbufs[i % 2][:, 0:KT * 128].rearrange("p (k c) -> p k c", k=KT)
                        for t2 in range(2):
                            t.wait_ge(sEV, C.get(sEV) + cum_banks(i - 1) + t2 + 1)
                            for k in range(KT):
                                mi = t.matmul(ps[t2][:, :],
                                              wbd[:, k, 0:128],
                                              sb_xTb[:, k * TOK + t2 * 512: k * TOK + (t2 + 1) * 512],
                                              start=(k == 0), stop=(k == KT - 1))
                            mi.then_inc(sMM, 1)

            @blk.scalar
            def _(sc):
                for i, (kind, idx) in enumerate(chunks):
                    stg = zstg[i % 2]
                    for bi in range(banks_per_chunk[i]):
                        sc.wait_ge(sMM, C.get(sMM) + cum_banks(i) + bi + 1)
                        if i >= 2 and bi == 0 and (i - 2) in spill_idx:
                            cSPz.wait(sc, spill_idx[i - 2])
                            if 16 <= i - 2 < 36:
                                sc.wait_ge(sDV, C.get(sDV) + (i - 2 - 16 + 1))
                        if kind == "z":
                            sc.activation(stg[:, bi * 512:(bi + 1) * 512], ps[bi][:, :],
                                          AF.Silu).then_inc(sEV, 1)
                        elif kind == "x":
                            sc.activation(stg[:, bi * 512:(bi + 1) * 512], ps[bi][:, :],
                                          AF.Copy).then_inc(sEV, 1)
                        else:
                            sc.activation(sb_dtraw[:, bi * 512:(bi + 1) * 512], ps[bi][:, :],
                                          AF.Copy).then_inc(sEV, 1)

            @blk.sync
            def _(s):
                for i, (kind, idx) in enumerate(chunks):
                    if kind == "dt":
                        continue
                    s.wait_ge(sEV, C.get(sEV) + cum_banks(i) + banks_per_chunk[i])
                    stg = zstg[i % 2]
                    if kind == "z":
                        cSPz.issue(s,
                                   z_sp[:, idx * 512:(idx + 1) * 512].rearrange("(tt p) c -> p tt c", p=128),
                                   stg[:, :].rearrange("p (tt c) -> p tt c", tt=8))
                    else:
                        cSPz.issue(s,
                                   xr_sp[idx * 512:(idx + 1) * 512, :].rearrange("(o p) t -> p o t", p=128),
                                   stg[:, :].rearrange("p (o t) -> p o t", o=4))

            @blk.vector
            def _(v):
                for i, (kind, idx) in enumerate(chunks):
                    if kind != "x":
                        continue
                    stg = zstg[i % 2]
                    for oct_ in range(4):
                        v.wait_ge(sEV, C.get(sEV) + cum_banks(i) + oct_ * 2 + 2)
                        g_tile = idx * 4 + oct_
                        ci = v.tensor_copy(sb_halo[:, g_tile * 3:(g_tile + 1) * 3],
                                           stg[:, (oct_ * 2 + 1) * 512 + 509: (oct_ * 2 + 1) * 512 + 512])
                    ci.then_inc(sDV, 1)
                cSPz.wait_last(v)

        C.add(sMM, cum_banks(nchunks))
        C.add(sEV, cum_banks(nchunks))
        C.add(sDV, 20)

        if "p1" in debug:
            with nc.Block() as blk:
                @blk.sync
                def _(s):
                    cH0.issue(s, dbg["z"][:, :], z_sp[:, :])
                    cH0.wait(s, len(cH0.marks) - 1)
                    cH0.issue(s, dbg["xr"][:, :], xr_sp[:, :])
                    cH0.wait(s, len(cH0.marks) - 1)
                    cH0.issue(s, dbg["dt"][:, :], sb_dtraw[:, :])
                    cH0.wait_last(s)

        if "p19" not in phases:
            raise _StopBuild()

        with nc.Block() as blk:
            hin_issue = len(cXC.marks)

            @blk.vector
            def _(v):
                v.tensor_scalar_mul(sb_halo[:, :], sb_halo[:, :], sb_role[:, 0:1]).then_inc(sDV, 1)
                cXC.wait(v, hin_issue + 1)
                v.tensor_scalar_mul(sb_halou[:, :], sb_halou[:, :], sb_role[:, 1:2]).then_inc(sDV, 1)

            @blk.sync
            def _(s):
                s.wait_ge(sDV, C.get(sDV) + 1)
                cXC.issue(s, h_in[:, :], sb_halo[:, :])

            @blk.gpsimd
            def _(g):
                cXC.wait(g, hin_issue)
                g.collective_compute("AllReduce", ALU.add,
                                     replica_groups=[[0, 1], [2, 3], [4, 5], [6, 7]],
                                     ins=[h_in.ap().opt()], outs=[h_out.ap().opt()]).then_inc(sCC, 1)

            @blk.scalar
            def _(sc):
                sc.wait_ge(sCC, C.get(sCC) + 1)
                cXC.issue(sc, sb_halou[:, :], h_out[:, :])
        C.add(sDV, 2)
        C.add(sCC, 1)

        if "p15" not in phases:
            raise _StopBuild()

        cin = [p1_ctx.enter_context(nc.sbuf_tensor(f"cv_in{i}", [128, 1027], BF16)) for i in range(2)]
        cdg = [p1_ctx.enter_context(nc.sbuf_tensor(f"cv_dg{i}", [128, 4 * 128], BF16)) for i in range(2)]
        cst = [p1_ctx.enter_context(nc.sbuf_tensor(f"cv_st{i}", [128, 1024], BF16)) for i in range(2)]
        NT = 80

        with nc.Block() as blk:
            @blk.sync
            def _(s):
                for o in range(NT):
                    if o >= 2:
                        s.wait_ge(sMM, C.get(sMM) + 2 * (o - 1))
                    cCV.issue(s, cin[o % 2][:, 3:1027], xr_sp[o * 128:(o + 1) * 128, :])

            @blk.vector
            def _(v):
                for o in range(NT):
                    cCV.wait(v, o)
                    if o >= 2:
                        v.wait_ge(sMM, C.get(sMM) + 2 * (o - 1))
                    v.tensor_copy(cin[o % 2][:, 0:3], sb_halou[:, o * 3:(o + 1) * 3])
                    for k in range(4):
                        di = v.tensor_scalar_mul(cdg[o % 2][:, k * 128:(k + 1) * 128],
                                                 sb_identb[:, :], sb_convw[:, o * 4 + k: o * 4 + k + 1])
                    di.then_inc(sDV, 1)

            @blk.tensor
            def _(t):
                for o in range(NT):
                    t.wait_ge(sDV, C.get(sDV) + o + 1)
                    for t2 in range(2):
                        if o >= 1:
                            t.wait_ge(sEV, C.get(sEV) + 2 * (o - 1) + t2 + 1)
                        for k in range(4):
                            mi = t.matmul(ps[t2][:, :],
                                          cdg[o % 2][:, k * 128:(k + 1) * 128],
                                          cin[o % 2][:, t2 * 512 + k: t2 * 512 + k + 512],
                                          start=(k == 0), stop=(k == 3))
                        mi.then_inc(sMM, 1)

            @blk.scalar
            def _(sc):
                for o in range(NT):
                    for t2 in range(2):
                        sc.wait_ge(sMM, C.get(sMM) + 2 * o + t2 + 1)
                        if o >= 2 and t2 == 0:
                            cCS.wait(sc, o - 2)
                        sc.activation(cst[o % 2][:, t2 * 512:(t2 + 1) * 512], ps[t2][:, :],
                                      AF.Silu, bias=sb_convb[:, o:o + 1]).then_inc(sEV, 1)

            @blk.gpsimd
            def _(g):
                for o in range(NT):
                    g.wait_ge(sEV, C.get(sEV) + 2 * o + 2)
                    cCS.issue(g, xc_sp[o * 128:(o + 1) * 128, :], cst[o % 2][:, :])
                cCS.wait_last(g)

        C.add(sDV, NT)
        C.add(sMM, 2 * NT)
        C.add(sEV, 2 * NT)

        if "p1" in debug:
            with nc.Block() as blk:
                @blk.sync
                def _(s):
                    cH0.issue(s, dbg["xc"][:, :], xc_sp[:, :])
                    cH0.wait_last(s)

        if "p2" not in phases:
            raise _StopBuild()

        _close(p1_ctx)

        p2_ctx = _mkstack(_cl)
        psT = [p2_ctx.enter_context(nc.psum_tensor(f"psT{i}", [128, 1024], BF16)) for i in range(2)]
        psD = [p2_ctx.enter_context(nc.psum_tensor(f"psD{i}", [128, 512], F32)) for i in range(2)]
        psY = [p2_ctx.enter_context(nc.psum_tensor(f"psY{i}", [128, 512], F32)) for i in range(2)]
        psS = p2_ctx.enter_context(nc.psum_tensor("psS", [128, 512], F32))
        psE = p2_ctx.enter_context(nc.psum_tensor("psE", [128, 512], F32))

        sb_xch1 = p2_ctx.enter_context(nc.sbuf_tensor("p2_x0", [128, 64 * 128], BF16))
        sb_xch = [sb_xch1, sb_xch1]
        sb_bcb = [p2_ctx.enter_context(nc.sbuf_tensor(f"p2_bc{i}", [128, 16 * 128], BF16)) for i in range(2)]
        x_tok = p2_ctx.enter_context(nc.sbuf_tensor("p2_xtok", [128, D_INNER], BF16))
        xdt = p2_ctx.enter_context(nc.sbuf_tensor("p2_xdt", [128, D_INNER], BF16))
        xD = p2_ctx.enter_context(nc.sbuf_tensor("p2_xD", [128, D_INNER], BF16))
        ystage = p2_ctx.enter_context(nc.sbuf_tensor("p2_yst", [128, D_INNER], BF16))
        B_tok = p2_ctx.enter_context(nc.sbuf_tensor("p2_btok", [128, 8 * 128], BF16))
        sb_Gm = p2_ctx.enter_context(nc.sbuf_tensor("p2_gm", [128, 8 * 128], BF16))
        FLn = p2_ctx.enter_context(nc.sbuf_tensor("p2_fln", [2, 16384], BF16))
        FLc = p2_ctx.enter_context(nc.sbuf_tensor("p2_flc", [2, 16384], BF16))
        sb_ones2 = p2_ctx.enter_context(nc.sbuf_tensor("p2_ones2", [2, 128], BF16))
        dtT_sb = p2_ctx.enter_context(nc.sbuf_tensor("p2_dtT", [128, 128], BF16))
        dsT_sb = p2_ctx.enter_context(nc.sbuf_tensor("p2_dsT", [128, 128], BF16))
        odT_sb = p2_ctx.enter_context(nc.sbuf_tensor("p2_odT", [128, 128], BF16))
        e_row = p2_ctx.enter_context(nc.sbuf_tensor("p2_erow", [1, 128], BF16))
        E_bc = p2_ctx.enter_context(nc.sbuf_tensor("p2_Ebc", [128, 128], BF16))
        sb_dsb = p2_ctx.enter_context(nc.sbuf_tensor("p2_dsb", [128, 128], BF16))
        sb_odb = p2_ctx.enter_context(nc.sbuf_tensor("p2_odb", [128, 128], BF16))
        sb_ecolb = p2_ctx.enter_context(nc.sbuf_tensor("p2_ecol", [128, 1], BF16))
        sb_dtb16 = p2_ctx.enter_context(nc.sbuf_tensor("p2_dtb16", [128, 128], BF16))
        sb_csh = p2_ctx.enter_context(nc.sbuf_tensor("p2_csh", [128, 128], BF16))
        sb_csl = p2_ctx.enter_context(nc.sbuf_tensor("p2_csl", [128, 128], BF16))
        sb_ncsh = p2_ctx.enter_context(nc.sbuf_tensor("p2_ncsh", [128, 128], BF16))
        sb_ncsl = p2_ctx.enter_context(nc.sbuf_tensor("p2_ncsl", [128, 128], BF16))
        Fb = [p2_ctx.enter_context(nc.sbuf_tensor(f"p2_F{i}", [128, 512], BF16)) for i in range(4)]
        Rb = [p2_ctx.enter_context(nc.sbuf_tensor(f"p2_R{i}", [128, 512], BF16)) for i in range(4)]
        sb_yo = [p2_ctx.enter_context(nc.sbuf_tensor(f"p2_yo{i}", [128, 512], BF16)) for i in range(2)]
        sb_sc = [p2_ctx.enter_context(nc.sbuf_tensor(f"p2_sc{i}", [128, 512], BF16)) for i in range(2)]
        sb_t2 = p2_ctx.enter_context(nc.sbuf_tensor("p2_t2", [128, 512], BF16))
        sb_t3 = sb_t2
        sb_xds = [p2_ctx.enter_context(nc.sbuf_tensor(f"p2_xds{i}", [128, 512], BF16)) for i in range(2)]
        sb_dA = sb_dtraw
        sb_tmpA = sb_cs
        sb_tmpB = sb_dtraw

        cL = Chain(nc, ctx, "cL", 2)
        cFL = Chain(nc, ctx, "cFL", 4)
        cYR = Chain(nc, ctx, "cYR", 2)

        with nc.Block() as blk:
            @blk.scalar
            def _(sc):
                sc.wait_ge(sDV, C.get(sDV) + 1)
                ev = C.get(sEV)
                sc.activation(sb_tmpA[:, :], sb_dt[:, :], AF.Abs).then_inc(sEV, 1)
                sc.wait_ge(sEV, ev + 1)
                sc.activation(sb_tmpA[:, 0:512], sb_tmpA[:, 0:512], AF.Exp, scale=-1.0).then_inc(sEV, 1)
                sc.activation(sb_tmpA[:, 512:1024], sb_tmpA[:, 512:1024], AF.Exp, scale=-1.0).then_inc(sEV, 1)
                sc.wait_ge(sEV, ev + 3)
                sc.activation(sb_tmpA[:, 0:512], sb_tmpA[:, 0:512], AF.Ln, bias=1.0).then_inc(sEV, 1)
                sc.activation(sb_tmpA[:, 512:1024], sb_tmpA[:, 512:1024], AF.Ln, bias=1.0).then_inc(sEV, 1)

            @blk.vector
            def _(v):
                v.tensor_scalar_add(sb_dt[:, :], sb_dtraw[:, :], sb_dtb[:, 0:1]).then_inc(sDV, 1)

                v.memset(sb_ones2[:, :], 1.0)
                v.memset(S_sb[:, :], 0.0)
                v.memset(psE[:, 0:128], 1.0)
                v.memset(sb_sig[:, :], 0.0)
                v.wait_ge(sEV, C.get(sEV) + 1)
                v.tensor_scalar_max(sb_tmpB[:, :], sb_dt[:, :], 0.0)
                v.tensor_add(sb_dt[:, :], sb_tmpB[:, :], sb_tmpA[:, :])
                v.tensor_scalar_mul(sb_dA[:, :], sb_dt[:, :], sb_A[:, 0:1])
                for k in range(NCH):
                    v.tensor_tensor_scan(sb_cs[:, k * CH:(k + 1) * CH], psE[:, 0:128],
                                         sb_dA[:, k * CH:(k + 1) * CH], 0.0, ALU.mult, ALU.add)
                for k in range(NCH):
                    vedge(v, v.tensor_add(sb_sig[:, k + 1:k + 2], sb_sig[:, k:k + 1],
                                          sb_cs[:, k * CH + CH - 1: k * CH + CH]))
        C.add(sEV, 5)
        C.add(sDV, 1)

        MM0 = C.get(sMM)
        EV0 = C.get(sEV)
        DV0 = C.get(sDV)
        MM_PC, EV_PC, DV_PC = 77, 76, 85

        def mmD(db):
            return 14 + db if db <= 5 else 22 + 4 * ((db - 6) // 2) + ((db - 6) % 2)

        def mmY(bb):
            return 20 + 4 * bb if bb <= 12 else 46 + 2 * bb

        def mmS(bb):
            return mmY(bb) + 1

        def evD(db):
            return 13 + db if db <= 5 else 21 + 4 * ((db - 6) // 2) + ((db - 6) % 2)

        def evYo(bb):
            return 19 + 4 * bb if bb <= 12 else 45 + 2 * bb

        def evSc(bb):
            return evYo(bb) + 1

        def dvR(db):
            return 6 + db if db <= 5 else 15 + 5 * ((db - 6) // 2) + ((db - 6) % 2)

        def dvXds(bb):
            return 12 + 5 * bb if bb <= 12 else 38 + 3 * bb

        def dvYadd(bb):
            return dvXds(bb) + 1

        def dvSupd(bb):
            return dvXds(bb) + 2

        def _w(eng, sem, base, val):
            if val > 0:
                eng.wait_ge(sem, base + val)

        fills = []
        fills.append([("dtT", 0), ("dsT", 0), ("odT", 0), ("ecolT", 0)] + [("B", g) for g in range(4)])
        fills.append([("B", g) for g in range(4, 8)] + [("x", o) for o in range(4)])
        for f in range(7):
            fills.append([("x", o) for o in range(4 + f * 8, 12 + f * 8)])
        fills.append([("x", o) for o in range(60, 64)])

        def emit_D(t, k, db):
            if db == 0:
                for fi in range(4):
                    cFL.wait(t, 4 * k + fi)
                _w(t, sDV, DV0, k * DV_PC + 2)
            elif db == 1:
                _w(t, sDV, DV0, k * DV_PC + 3)
            else:
                _w(t, sEV, EV0, k * EV_PC + evD(db - 2))
            bank = psD[db % 2]
            for hh in range(4):
                h = db * 4 + hh
                sl = slice(hh * 128, (hh + 1) * 128)
                t.matmul(bank[:, sl], sb_ones2[:, :], FLc[0:2, h * 128:(h + 1) * 128],
                         start=True, stop=False)
                mi = t.matmul(bank[:, sl], FLn[0:2, h * 128:(h + 1) * 128], sb_ones2[:, :],
                              start=False, stop=True)
            mi.then_inc(sMM, 1)

        def emit_Y(t, k, bb):
            g_ = bb // 2
            bcb = sb_bcb[k % 2]
            _w(t, sDV, DV0, k * DV_PC + dvR(2 * bb + 1))
            _w(t, sDV, DV0, (k - 1) * DV_PC + dvSupd(bb))
            if bb >= 1:
                _w(t, sEV, EV0, k * EV_PC + evYo(bb - 1))
            else:
                _w(t, sEV, EV0, k * EV_PC + 12)
            t.matmul(psE[:, :], bcb[:, (8 + g_) * 128:(9 + g_) * 128],
                     S_sb[:, bb * 512:(bb + 1) * 512], start=True, stop=True)
            if bb >= 2:
                _w(t, sDV, DV0, k * DV_PC + dvYadd(bb - 2))
            else:
                _w(t, sDV, DV0, (k - 1) * DV_PC + dvYadd(bb + 14))
            yb = psY[bb % 2]
            for hh in range(8):
                h = bb * 8 + hh
                db = h // 4
                t.matmul(yb[:, hh * 64:(hh + 1) * 64],
                         Rb[db % 4][:, (h % 4) * 128:(h % 4 + 1) * 128],
                         xdt[:, h * 64:(h + 1) * 64],
                         start=(hh == 0), stop=False, skip_group_check=True)
            t.matmul(yb[:, :], sb_identb[:, :], xD[:, bb * 512:(bb + 1) * 512],
                     start=False, stop=True, skip_group_check=True).then_inc(sMM, 1)

        def emit_S(t, k, bb):
            g_ = bb // 2
            _w(t, sDV, DV0, k * DV_PC + dvXds(bb))
            if bb >= 1:
                _w(t, sEV, EV0, k * EV_PC + evSc(bb - 1))
            else:
                _w(t, sEV, EV0, (k - 1) * EV_PC + 76)
            t.matmul(psS[:, :], B_tok[:, g_ * 128:(g_ + 1) * 128],
                     sb_xds[bb % 2][:, :], start=True, stop=True).then_inc(sMM, 1)

        with nc.Block() as blk:
            @blk.sync
            def _(s):
                for k in range(NCH):
                    if k >= 1:
                        _w(s, sMM, MM0, (k - 1) * MM_PC + 11)
                    if k >= 2:
                        _w(s, sMM, MM0, (k - 1) * MM_PC)
                    cL.issue(s, sb_xch[k % 2][:, :].rearrange("p (o t) -> p o t", o=64),
                             xc_sp[0:D_INNER, k * CH:(k + 1) * CH].rearrange("(o p) t -> p o t", p=128))
                    cL.issue(s, sb_bcb[k % 2][:, :].rearrange("p (o t) -> p o t", o=16),
                             xc_sp[D_INNER:XBC_OC, k * CH:(k + 1) * CH].rearrange("(o p) t -> p o t", p=128))
                    _w(s, sDV, DV0, k * DV_PC + 1)
                    if k >= 1:
                        _w(s, sMM, MM0, (k - 1) * MM_PC + mmD(31))
                    cFL.issue(s, FLn[0:1, :], sb_ncsh[:, :])
                    cFL.issue(s, FLn[1:2, :], sb_ncsl[:, :])
                    cFL.issue(s, FLc[0:1, :], sb_csh[:, :])
                    cFL.issue(s, FLc[1:2, :], sb_csl[:, :])

            @blk.gpsimd
            def _(g):
                for k in range(NCH):
                    _w(g, sDV, DV0, k * DV_PC + dvYadd(15))
                    cYR.issue(g, yr_sp[k * CH:(k + 1) * CH, :], ystage[:, :])
                cYR.wait_last(g)

            @blk.tensor
            def _(t):
                for k in range(NCH):
                    xch = sb_xch[k % 2]
                    bcb = sb_bcb[k % 2]
                    for j, fill in enumerate(fills):
                        if j == 0:
                            cL.wait(t, 2 * k)
                            cL.wait(t, 2 * k + 1)
                            _w(t, sDV, DV0, k * DV_PC + 1)
                            _w(t, sEV, EV0, k * EV_PC + 1)
                            _w(t, sEV, EV0, (k - 1) * EV_PC + 10)
                        elif j == 1:
                            _w(t, sEV, EV0, (k - 1) * EV_PC + 11)
                        else:
                            _w(t, sEV, EV0, k * EV_PC + j)
                        bank = psT[j % 2]
                        for sl, (kind, idx) in enumerate(fill):
                            dst = bank[:, sl * 128:(sl + 1) * 128]
                            if kind == "dtT":
                                mi = t.transpose(dst, sb_dtb16[:, :], sb_identb[:, :])
                            elif kind == "dsT":
                                mi = t.transpose(dst, sb_dsb[:, :], sb_identb[:, :])
                            elif kind == "odT":
                                mi = t.transpose(dst, sb_odb[:, :], sb_identb[:, :])
                            elif kind == "ecolT":
                                mi = t.transpose(bank[0:1, sl * 128:(sl + 1) * 128], sb_ecolb[:, :], sb_identb[:, :])
                            elif kind == "B":
                                mi = t.transpose(dst, bcb[:, idx * 128:(idx + 1) * 128], sb_identb[:, :])
                            else:
                                mi = t.transpose(dst, xch[:, idx * 128:(idx + 1) * 128], sb_identb[:, :])
                        mi.then_inc(sMM, 1)
                        if j == 1:
                            _w(t, sEV, EV0, k * EV_PC + 2)
                            _w(t, sEV, EV0, (k - 1) * EV_PC + 12)
                            t.matmul(psE[:, 0:128], sb_ones1[:, :], e_row[:, :],
                                     start=True, stop=True).then_inc(sMM, 1)
                    for half in range(2):
                        _w(t, sEV, EV0, (k - 1) * EV_PC + evD(30 + half))
                        for gg in range(4):
                            g_ = half * 4 + gg
                            mi = t.matmul(psD[half][:, gg * 128:(gg + 1) * 128],
                                          bcb[:, g_ * 128:(g_ + 1) * 128],
                                          bcb[:, (8 + g_) * 128:(9 + g_) * 128],
                                          start=True, stop=True, skip_group_check=True)
                        mi.then_inc(sMM, 1)
                    for db in range(6):
                        emit_D(t, k, db)
                    for bb in range(16):
                        emit_Y(t, k, bb)
                        emit_S(t, k, bb)
                        if bb <= 12:
                            for dd in range(2):
                                db = 6 + 2 * bb + dd
                                if db < 32:
                                    emit_D(t, k, db)

            @blk.scalar
            def _(sc):
                for k in range(NCH):
                    ch = slice(k * CH, (k + 1) * CH)
                    last = slice(k * CH + CH - 1, k * CH + CH)
                    _w(sc, sMM, MM0, (k - 1) * MM_PC + 1)
                    sc.activation(sb_dsb[:, :], sb_cs[:, ch], AF.Exp, scale=-1.0, bias=sb_cs[:, last])
                    sc.activation(sb_odb[:, :], sb_cs[:, ch], AF.Exp)
                    sc.activation(sb_ecolb[:, :], sb_cs[:, last], AF.Exp).then_inc(sEV, 1)
                    for j, fill in enumerate(fills):
                        _w(sc, sMM, MM0, k * MM_PC + j + 1 + (1 if j >= 2 else 0))
                        if j == 0:
                            _w(sc, sDV, DV0, (k - 1) * DV_PC + 85)
                        bank = psT[j % 2]
                        if j == 0:
                            dsts = [dtT_sb[:, :], dsT_sb[:, :], odT_sb[:, :], None] + \
                                   [B_tok[:, gg * 128:(gg + 1) * 128] for gg in range(4)]
                            for sl, d in enumerate(dsts):
                                if d is None:
                                    mi = sc.activation(e_row[:, :], bank[0:1, sl * 128:(sl + 1) * 128], AF.Copy)
                                else:
                                    mi = sc.activation(d, bank[:, sl * 128:(sl + 1) * 128], AF.Copy)
                        elif j == 1:
                            for sl in range(4):
                                mi = sc.activation(B_tok[:, (4 + sl) * 128:(5 + sl) * 128],
                                                   bank[:, sl * 128:(sl + 1) * 128], AF.Copy)
                            mi = sc.activation(x_tok[:, 0:512], bank[:, 512:1024], AF.Copy)
                        elif j < 9:
                            o0 = 4 + (j - 2) * 8
                            mi = sc.activation(x_tok[:, o0 * 128:(o0 + 8) * 128], bank[:, :], AF.Copy)
                        else:
                            mi = sc.activation(x_tok[:, 60 * 128:64 * 128], bank[:, 0:512], AF.Copy)
                        mi.then_inc(sEV, 1)
                    _w(sc, sMM, MM0, k * MM_PC + 3)
                    _w(sc, sDV, DV0, (k - 1) * DV_PC + 85)
                    sc.activation(E_bc[:, :], psE[:, 0:128], AF.Copy).then_inc(sEV, 1)

                    def emit_Dexp(db):
                        _w(sc, sMM, MM0, k * MM_PC + mmD(db))
                        if db >= 4:
                            _w(sc, sDV, DV0, k * DV_PC + dvR(db - 4))
                        else:
                            _w(sc, sDV, DV0, (k - 1) * DV_PC + dvR(28 + db))
                        sc.activation(Fb[db % 4][:, :], psD[db % 2][:, :], AF.Exp).then_inc(sEV, 1)

                    for db in range(6):
                        emit_Dexp(db)
                    for bb in range(16):
                        _w(sc, sMM, MM0, k * MM_PC + mmY(bb))
                        if bb >= 2:
                            _w(sc, sDV, DV0, k * DV_PC + dvYadd(bb - 2))
                        else:
                            _w(sc, sDV, DV0, (k - 1) * DV_PC + dvYadd(bb + 14))
                        sc.activation(sb_yo[bb % 2][:, :], psE[:, :], AF.Copy).then_inc(sEV, 1)
                        _w(sc, sMM, MM0, k * MM_PC + mmS(bb))
                        if bb >= 2:
                            _w(sc, sDV, DV0, k * DV_PC + dvSupd(bb - 2))
                        else:
                            _w(sc, sDV, DV0, (k - 1) * DV_PC + dvSupd(bb + 14))
                        sc.activation(sb_sc[bb % 2][:, :], psS[:, :], AF.Copy).then_inc(sEV, 1)
                        if bb <= 12:
                            for dd in range(2):
                                db = 6 + 2 * bb + dd
                                if db < 32:
                                    emit_Dexp(db)

            @blk.vector
            def _(v):
                for k in range(NCH):
                    ch = slice(k * CH, (k + 1) * CH)
                    _w(v, sMM, MM0, (k - 1) * MM_PC + 1)
                    if k >= 1:
                        for fi in range(4):
                            cFL.wait(v, 4 * (k - 1) + fi)
                    v.tensor_copy(sb_dtb16[:, :], sb_dt[:, ch])
                    v.tensor_copy(sb_csh[:, :], sb_cs[:, ch])
                    v.tensor_tensor(sb_csl[:, :], sb_cs[:, ch], sb_csh[:, :], ALU.subtract)
                    v.tensor_scalar_mul(sb_ncsh[:, :], sb_csh[:, :], -1.0)
                    v.tensor_scalar_mul(sb_ncsl[:, :], sb_csl[:, :], -1.0).then_inc(sDV, 1)
                    for half in range(2):
                        _w(v, sMM, MM0, k * MM_PC + 12 + half)
                        for gg in range(4):
                            g_ = half * 4 + gg
                            mi = v.tensor_tensor(sb_Gm[:, g_ * 128:(g_ + 1) * 128],
                                                 psD[half][:, gg * 128:(gg + 1) * 128],
                                                 sb_maskb[:, :], ALU.mult)
                        mi.then_inc(sDV, 1)
                    _w(v, sEV, EV0, k * EV_PC + 11)
                    _w(v, sMM, MM0, (k - 1) * MM_PC + 77)
                    x3 = x_tok[:, :].rearrange("p (h q) -> p h q", h=128)
                    v.tensor_tensor(xdt[:, :].rearrange("p (h q) -> p h q", h=128), x3,
                                    dtT_sb[:, :].unsqueeze(2).broadcast_to([128, 128, 64]),
                                    ALU.mult).then_inc(sDV, 1)
                    if k >= 1:
                        cYR.wait(v, k - 1)
                    v.tensor_tensor(xD[:, :].rearrange("p (h q) -> p h q", h=128), x3,
                                    sb_Db[:, :].unsqueeze(2).broadcast_to([128, 128, 64]),
                                    ALU.mult).then_inc(sDV, 1)

                    def emit_R(db):
                        _w(v, sEV, EV0, k * EV_PC + evD(db))
                        if db >= 4:
                            _w(v, sMM, MM0, k * MM_PC + mmY((db - 4) // 2))
                        else:
                            _w(v, sMM, MM0, (k - 1) * MM_PC + mmY(14 + db // 2))
                        g_ = db // 4
                        v.scalar_tensor_tensor(
                            Rb[db % 4][:, :].rearrange("p (a b) -> p a b", a=4),
                            Fb[db % 4][:, :].rearrange("p (a b) -> p a b", a=4), 1.0,
                            sb_Gm[:, g_ * 128:(g_ + 1) * 128].unsqueeze(1).broadcast_to([128, 4, 128]),
                            ALU.min, ALU.mult).then_inc(sDV, 1)

                    for db in range(6):
                        emit_R(db)
                    for bb in range(16):
                        g_ = bb // 2
                        sl = slice(bb * 512, (bb + 1) * 512)
                        if bb >= 2:
                            _w(v, sMM, MM0, k * MM_PC + mmS(bb - 2))
                        else:
                            _w(v, sMM, MM0, (k - 1) * MM_PC + mmS(bb + 14))
                        v.tensor_tensor(sb_xds[bb % 2][:, :].rearrange("p (a b) -> p a b", a=8),
                                        xdt[:, sl].rearrange("p (a b) -> p a b", a=8),
                                        dsT_sb[:, bb * 8:(bb + 1) * 8].unsqueeze(2).broadcast_to([128, 8, 64]),
                                        ALU.mult).then_inc(sDV, 1)
                        _w(v, sEV, EV0, k * EV_PC + evYo(bb))
                        v.tensor_tensor(sb_t2[:, :].rearrange("p (a b) -> p a b", a=8),
                                        sb_yo[bb % 2][:, :].rearrange("p (a b) -> p a b", a=8),
                                        odT_sb[:, bb * 8:(bb + 1) * 8].unsqueeze(2).broadcast_to([128, 8, 64]),
                                        ALU.mult)
                        v.tensor_tensor(ystage[:, sl], psY[bb % 2][:, :], sb_t2[:, :],
                                        ALU.add).then_inc(sDV, 1)
                        _w(v, sEV, EV0, k * EV_PC + evSc(bb))
                        _w(v, sMM, MM0, k * MM_PC + mmY(bb))
                        v.tensor_tensor(sb_t3[:, :].rearrange("p (a b) -> p a b", a=8),
                                        S_sb[:, sl].rearrange("p (a b) -> p a b", a=8),
                                        E_bc[:, bb * 8:(bb + 1) * 8].unsqueeze(2).broadcast_to([128, 8, 64]),
                                        ALU.mult)
                        v.tensor_tensor(S_sb[:, sl], sb_t3[:, :], sb_sc[bb % 2][:, :],
                                        ALU.add).then_inc(sDV, 1)
                        if bb <= 12:
                            for dd in range(2):
                                db = 6 + 2 * bb + dd
                                if db < 32:
                                    emit_R(db)

        C.add(sMM, NCH * MM_PC)
        C.add(sEV, NCH * EV_PC)
        C.add(sDV, NCH * DV_PC)

        if "p2" in debug:
            with nc.Block() as blk:
                @blk.sync
                def _(s):
                    cH0.issue(s, dbg["yr"][:, :], yr_sp[:, :])
                    cH0.issue(s, dbg["st"][:, :], S_sb[:, :])
                    cH0.wait_last(s)

        if "p29" not in phases:
            raise _StopBuild()

        with nc.Block() as blk:
            st_issue = len(cXC.marks)

            @blk.vector
            def _(v):
                v.tensor_scalar_mul(S_sb[:, :], S_sb[:, :], sb_role[:, 0:1]).then_inc(sDV, 1)
                cXC.wait(v, st_issue + 1)
                v.tensor_scalar_mul(S_sb[:, :], S_sb[:, :], sb_role[:, 1:2]).then_inc(sDV, 1)

            @blk.sync
            def _(s):
                s.wait_ge(sDV, C.get(sDV) + 1)
                cXC.issue(s, st_in[:, :], S_sb[:, :])

            @blk.gpsimd
            def _(g):
                cXC.wait(g, st_issue)
                g.collective_compute("AllReduce", ALU.add,
                                     replica_groups=[[0, 1], [2, 3], [4, 5], [6, 7]],
                                     ins=[st_in.ap().opt()], outs=[st_out.ap().opt()]).then_inc(sCC, 1)

            @blk.scalar
            def _(sc):
                sc.wait_ge(sCC, C.get(sCC) + 1)
                cXC.issue(sc, S_sb[:, :], st_out[:, :])
        C.add(sDV, 2)
        C.add(sCC, 1)

        if "p2c" not in phases:
            raise _StopBuild()

        _close(p2_ctx)

        p2c_ctx = _mkstack(_cl)
        cT = [p2c_ctx.enter_context(nc.psum_tensor(f"cT{i}", [128, 1024], BF16)) for i in range(2)]
        cP = [p2c_ctx.enter_context(nc.psum_tensor(f"cP{i}", [128, 512], F32)) for i in range(2)]
        ybuf = [p2c_ctx.enter_context(nc.sbuf_tensor(f"c2_y{i}", [128, D_INNER], BF16)) for i in range(2)]
        zbuf = [p2c_ctx.enter_context(nc.sbuf_tensor(f"c2_z{i}", [128, D_INNER], BF16)) for i in range(2)]
        cbuf = [p2c_ctx.enter_context(nc.sbuf_tensor(f"c2_c{i}", [128, 8 * 128], BF16)) for i in range(2)]
        yT_acc = p2c_ctx.enter_context(nc.sbuf_tensor("c2_yT", [128, 64 * 512], BF16))
        odb2 = p2c_ctx.enter_context(nc.sbuf_tensor("c2_od", [128, 128], BF16))
        odT2 = p2c_ctx.enter_context(nc.sbuf_tensor("c2_odT", [128, 128], BF16))
        cstg = [p2c_ctx.enter_context(nc.sbuf_tensor(f"c2_cs{i}", [128, 512], BF16)) for i in range(2)]
        ct2 = p2c_ctx.enter_context(nc.sbuf_tensor("c2_t2", [128, 512], BF16))
        ssum = p2c_ctx.enter_context(nc.sbuf_tensor("c2_ss", [128, 1], F32))

        cL2 = Chain(nc, ctx, "cL2", 3)
        cYT = Chain(nc, ctx, "cYT", 2)

        MM2, EV2, DV2 = 25, 28, 17
        M2 = C.get(sMM)
        E2 = C.get(sEV)
        D2 = C.get(sDV)

        with nc.Block() as blk:
            @blk.sync
            def _(s):
                for k in range(NCH):
                    if k >= 2:
                        _w(s, sMM, M2, (k - 2) * MM2 + 25)
                        _w(s, sEV, E2, (k - 2) * EV2 + 27)
                    cL2.issue(s, ybuf[k % 2][:, :], yr_sp[k * CH:(k + 1) * CH, :])
                    cL2.issue(s, zbuf[k % 2][:, :], z_sp[k * CH:(k + 1) * CH, :])
                    cL2.issue(s, cbuf[k % 2][:, :].rearrange("p (o t) -> p o t", o=8),
                              xc_sp[D_INNER + 1024:XBC_OC, k * CH:(k + 1) * CH].rearrange("(o p) t -> p o t", p=128))

            @blk.tensor
            def _(t):
                for k in range(NCH):
                    _w(t, sEV, E2, k * EV2 + 1)
                    _w(t, sEV, E2, (k - 1) * EV2 + 25)
                    t.transpose(cT[0][:, 0:128], odb2[:, :], sb_identb[:, :]).then_inc(sMM, 1)
                    for bb in range(16):
                        g_ = bb // 2
                        if bb == 0:
                            cL2.wait(t, 3 * k + 2)
                        if bb >= 2:
                            _w(t, sEV, E2, k * EV2 + 1 + bb)
                        else:
                            _w(t, sEV, E2, (k - 1) * EV2 + 17 + bb)
                        t.matmul(cP[bb % 2][:, :], cbuf[k % 2][:, g_ * 128:(g_ + 1) * 128],
                                 S_sb[:, bb * 512:(bb + 1) * 512], start=True, stop=True).then_inc(sMM, 1)
                    _w(t, sDV, D2, k * DV2 + 17)
                    for f in range(8):
                        if f == 0:
                            _w(t, sEV, E2, k * EV2 + 2)
                        elif f == 1:
                            _w(t, sEV, E2, (k - 1) * EV2 + 26)
                        else:
                            _w(t, sEV, E2, k * EV2 + 19 + (f - 2))
                        for q in range(8):
                            o = f * 8 + q
                            mi = t.transpose(cT[f % 2][:, q * 128:(q + 1) * 128],
                                             ybuf[k % 2][:, o * 128:(o + 1) * 128], sb_identb[:, :])
                        mi.then_inc(sMM, 1)

            @blk.scalar
            def _(sc):
                for k in range(NCH):
                    ch = slice(k * CH, (k + 1) * CH)
                    _w(sc, sMM, M2, (k - 1) * MM2 + 1)
                    sc.activation(odb2[:, :], sb_cs[:, ch], AF.Exp, bias=sb_sig[:, k:k + 1]).then_inc(sEV, 1)
                    _w(sc, sMM, M2, k * MM2 + 1)
                    _w(sc, sDV, D2, (k - 1) * DV2 + 16)
                    sc.activation(odT2[:, :], cT[0][:, 0:128], AF.Copy).then_inc(sEV, 1)
                    for bb in range(16):
                        _w(sc, sMM, M2, k * MM2 + 2 + bb)
                        if bb >= 2:
                            _w(sc, sDV, D2, k * DV2 + bb - 1)
                        else:
                            _w(sc, sDV, D2, (k - 1) * DV2 + 15 + bb)
                        sc.activation(cstg[bb % 2][:, :], cP[bb % 2][:, :], AF.Copy).then_inc(sEV, 1)
                    for f in range(8):
                        _w(sc, sMM, M2, k * MM2 + 18 + f)
                        if f == 0 and k >= 4:
                            cYT.wait(sc, k // 4 - 1)
                        dst = yT_acc[:, :].rearrange("p (o kc t) -> p o kc t", o=64, kc=4)[:, f * 8:(f + 1) * 8, k % 4, :]
                        sc.activation(dst, cT[f % 2][:, :].rearrange("p (o t) -> p o t", o=8),
                                      AF.Copy).then_inc(sEV, 1)
                    _w(sc, sDV, D2, k * DV2 + 17)
                    sc.activation(zbuf[k % 2][:, :], ybuf[k % 2][:, :], AF.Square,
                                  accum_out=ssum[:, 0:1]).then_inc(sEV, 1)
                    sc.wait_ge(sEV, E2 + k * EV2 + 27)
                    sc.activation(sb_rs[:, k:k + 1], ssum[:, 0:1], AF.Ln,
                                  scale=1.0 / D_INNER, bias=sb_eps[:, 0:1]).then_inc(sEV, 1)
                    sc.wait_ge(sEV, E2 + k * EV2 + 28)
                    sc.activation(sb_rs[:, k:k + 1], sb_rs[:, k:k + 1], AF.Exp, scale=-0.5)

            @blk.vector
            def _(v):
                for k in range(NCH):
                    cL2.wait(v, 3 * k)
                    for bb in range(16):
                        sl = slice(bb * 512, (bb + 1) * 512)
                        _w(v, sEV, E2, k * EV2 + 3 + bb)
                        v.tensor_tensor(ct2[:, :].rearrange("p (a b) -> p a b", a=8),
                                        cstg[bb % 2][:, :].rearrange("p (a b) -> p a b", a=8),
                                        odT2[:, bb * 8:(bb + 1) * 8].unsqueeze(2).broadcast_to([128, 8, 64]),
                                        ALU.mult)
                        v.tensor_tensor(ybuf[k % 2][:, sl], ybuf[k % 2][:, sl], ct2[:, :],
                                        ALU.add).then_inc(sDV, 1)
                    cL2.wait(v, 3 * k + 1)
                    v.tensor_tensor(ybuf[k % 2][:, :], ybuf[k % 2][:, :], zbuf[k % 2][:, :],
                                    ALU.mult).then_inc(sDV, 1)

            @blk.gpsimd
            def _(g):
                for hf in range(2):
                    _w(g, sEV, E2, (4 * hf + 3) * EV2 + 26)
                    cYT.issue(g, yT_sp[:, hf * 512:(hf + 1) * 512].rearrange("(o p) t -> p o t", p=128),
                              yT_acc[:, :].rearrange("p (o t) -> p o t", o=64))
                cYT.wait_last(g)

        C.add(sMM, NCH * MM2)
        C.add(sEV, NCH * EV2)
        C.add(sDV, NCH * DV2)

        if "p2c" in debug:
            with nc.Block() as blk:
                @blk.sync
                def _(s):
                    cH0.issue(s, dbg["yT"][:, :], yT_sp[:, :])
                    cH0.wait_last(s)

        if "p3" not in phases:
            raise _StopBuild()

        _close(p2c_ctx)
        p3_ctx = _mkstack(_cl)
        psO = [p3_ctx.enter_context(nc.psum_tensor(f"psO{i}", [128, 512], F32)) for i in range(8)]
        yTh = p3_ctx.enter_context(nc.sbuf_tensor("p3_yT", [128, 64 * 512], BF16))
        wobuf = [p3_ctx.enter_context(nc.sbuf_tensor(f"p3_w{i}", [128, 16 * 512], BF16)) for i in range(2)]
        xresb = p3_ctx.enter_context(nc.sbuf_tensor("p3_xr", [128, 4 * 4096], BF16))
        x1stg = [p3_ctx.enter_context(nc.sbuf_tensor(f"p3_s{i}", [128, 4 * 512], BF16)) for i in range(2)]
        p3tmp = p3_ctx.enter_context(nc.sbuf_tensor("p3_tmp", [128, 512], BF16))

        cYL = Chain(nc, ctx, "cYL", 2)
        cWO = Chain(nc, ctx, "cWO", 2)
        cXR = Chain(nc, ctx, "cXR", 2)
        cX1 = Chain(nc, ctx, "cX1", 2)

        M3 = C.get(sMM)
        D3 = C.get(sDV)
        MM3, DV3 = 32, 33

        with nc.Block() as blk:
            @blk.sync
            def _(s):
                for hf in range(2):
                    if hf >= 1:
                        _w(s, sMM, M3, hf * MM3)
                    cYL.issue(s, yTh[:, :].rearrange("p (o t) -> p o t", o=64),
                              yT_sp[:, hf * 512:(hf + 1) * 512].rearrange("(o p) t -> p o t", p=128))
                    for ob in range(8):
                        _w(s, sDV, D3, hf * DV3 + 1 + 4 * (ob + 1))
                        cX1.issue(s,
                                  x1_sp[hf * 512:(hf + 1) * 512, ob * 512:(ob + 1) * 512].rearrange("(tt p) c -> p tt c", p=128),
                                  x1stg[ob % 2][:, :].rearrange("p (tt c) -> p tt c", tt=4))

            @blk.gpsimd
            def _(g):
                for hf in range(2):
                    if hf >= 1:
                        _w(g, sDV, D3, hf * DV3)
                    cXR.issue(g, xresb[:, :].rearrange("p (tt c) -> p tt c", tt=4),
                              d_xres[hf * 512:(hf + 1) * 512, :].rearrange("(tt p) c -> p tt c", p=128))
                    for ob in range(8):
                        for cq in range(4):
                            bi = ob * 4 + cq
                            if bi >= 2 or hf >= 1:
                                _w(g, sMM, M3, hf * MM3 + bi - 1)
                            cWO.issue(g, wobuf[bi % 2][:, :].rearrange("p (k c) -> p k c", k=16),
                                      d_wo[cq * 2048:(cq + 1) * 2048, ob * 512:(ob + 1) * 512].rearrange("(k p) c -> p k c", p=128))

            @blk.vector
            def _(v):
                for hf in range(2):
                    cYL.wait(v, hf)
                    for ct in range(64):
                        mi = v.tensor_scalar_mul(yTh[:, ct * 512:(ct + 1) * 512],
                                                 yTh[:, ct * 512:(ct + 1) * 512], sb_nrmw[:, ct:ct + 1])
                    mi.then_inc(sDV, 1)
                    cXR.wait(v, hf)
                    for ob in range(8):
                        for tt in range(4):
                            _w(v, sMM, M3, hf * MM3 + ob * 4 + 4)
                            if ob >= 2 and tt == 0:
                                cX1.wait(v, hf * 8 + ob - 2)
                            v.tensor_scalar_mul(p3tmp[:, :], psO[(ob % 2) * 4 + tt][:, :],
                                                sb_rs[:, hf * 4 + tt: hf * 4 + tt + 1])
                            v.scalar_tensor_tensor(
                                x1stg[ob % 2][:, tt * 512:(tt + 1) * 512],
                                xresb[:, tt * 4096 + ob * 512: tt * 4096 + (ob + 1) * 512], 2.0,
                                p3tmp[:, :], ALU.mult, ALU.add).then_inc(sDV, 1)

            @blk.tensor
            def _(t):
                for hf in range(2):
                    _w(t, sDV, D3, hf * DV3 + 1)
                    for ob in range(8):
                        for cq in range(4):
                            bi = ob * 4 + cq
                            cWO.wait(t, hf * 32 + bi)
                            wb = wobuf[bi % 2][:, :].rearrange("p (k c) -> p k c", k=16)
                            for kk in range(16):
                                ct = cq * 16 + kk
                                for tt in range(4):
                                    bank = psO[(ob % 2) * 4 + tt]
                                    if cq == 0 and kk == 0 and ob >= 2:
                                        _w(t, sDV, D3, hf * DV3 + 1 + 4 * (ob - 2) + tt + 1)
                                    mi = t.matmul(bank[:, :],
                                                  yTh[:, ct * 512 + tt * 128: ct * 512 + (tt + 1) * 128],
                                                  wb[:, kk, :],
                                                  start=(cq == 0 and kk == 0),
                                                  stop=(cq == 3 and kk == 15))
                            mi.then_inc(sMM, 1)

        C.add(sMM, 2 * MM3)
        C.add(sDV, 2 * DV3)

        if "p3" in debug:
            with nc.Block() as blk:
                @blk.sync
                def _(s):
                    cH0.issue(s, dbg["x1"][:, :], x1_sp[:, :])
                    cH0.wait_last(s)

        if "p4" not in phases:
            raise _StopBuild()

        _close(p3_ctx)
        p4_ctx = _mkstack(_cl)
        x1T = p4_ctx.enter_context(nc.sbuf_tensor("p4_x1T", [128, KT * TOK], BF16))
        wgb = p4_ctx.enter_context(nc.sbuf_tensor("p4_wg", [128, KT * 8], BF16))
        aTb = p4_ctx.enter_context(nc.sbuf_tensor("p4_aT", [128, KT * 128], BF16))
        bTfb = p4_ctx.enter_context(nc.sbuf_tensor("p4_bT", [128, D_MODEL], BF16))
        sb_bbase = p4_ctx.enter_context(nc.sbuf_tensor("p4_bb", [1, D_MODEL], BF16))
        logtok = p4_ctx.enter_context(nc.sbuf_tensor("p4_lg", [128, 64], F32))
        route = p4_ctx.enter_context(nc.sbuf_tensor("p4_rt", [128, 64], BF16))
        r_m1 = p4_ctx.enter_context(nc.sbuf_tensor("p4_m1", [128, 8], F32))
        r_sc = p4_ctx.enter_context(nc.sbuf_tensor("p4_sc", [128, 8 * 4], F32))
        hwT = p4_ctx.enter_context(nc.sbuf_tensor("p4_hwT", [128, TOK], BF16))
        hwstg = p4_ctx.enter_context(nc.sbuf_tensor("p4_hws", [128, 128], BF16))
        p4a_ctx = _mkstack(_cl)
        pT4 = [p4a_ctx.enter_context(nc.psum_tensor(f"pT4{i}", [128, 1024], BF16)) for i in range(2)]
        pL4 = [p4a_ctx.enter_context(nc.psum_tensor(f"pL4{i}", [128, 512], F32)) for i in range(2)]
        x1tl = [p4a_ctx.enter_context(nc.sbuf_tensor(f"p4_x1{i}", [128, D_MODEL], BF16)) for i in range(2)]
        x1ntl = [p4a_ctx.enter_context(nc.sbuf_tensor(f"p4_xn{i}", [128, D_MODEL], BF16)) for i in range(2)]
        dump4 = p4a_ctx.enter_context(nc.sbuf_tensor("p4_dump", [128, D_MODEL], BF16))
        s1c = p4a_ctx.enter_context(nc.sbuf_tensor("p4_s1", [128, 1], F32))
        s2c = p4a_ctx.enter_context(nc.sbuf_tensor("p4_s2", [128, 1], F32))

        cX2 = Chain(nc, ctx, "cX2", 2)
        cW4 = Chain(nc, ctx, "cW4", 2)

        with nc.Block() as blk:
            @blk.gpsimd
            def _(g):
                cW4.issue(g, wgb[:, :], d_wg[:, :])
                cW4.issue(g, aTb[:, :], d_aT[:, :])
                cW4.issue(g, bTfb[:, :], d_bTf[:, :])
                cW4.issue(g, sb_bbase[:, :], d_bbase[:, :])

            @blk.sync
            def _(s):
                e0 = C.get(sEV)
                d0 = C.get(sDV)
                for tt in range(8):
                    if tt >= 2:
                        _w(s, sEV, e0, 8 * (tt - 2) + 2)
                        _w(s, sDV, d0, 2 * (tt - 2) + 2)
                    cX2.issue(s, x1tl[tt % 2][:, :], x1_sp[tt * 128:(tt + 1) * 128, :])

            @blk.scalar
            def _(sc):
                e0 = C.get(sEV)
                m0 = C.get(sMM)
                d0 = C.get(sDV)
                for tt in range(8):
                    cX2.wait(sc, tt)
                    if tt >= 1:
                        _w(sc, sDV, d0, 2 * (tt - 1) + 1)
                    sc.activation(dump4[:, :], x1tl[tt % 2][:, :], AF.Copy,
                                  accum_out=s1c[:, 0:1]).then_inc(sEV, 1)
                    sc.wait_ge(sEV, e0 + 8 * tt + 1)
                    sc.activation(dump4[:, :], x1tl[tt % 2][:, :], AF.Square,
                                  accum_out=s2c[:, 0:1]).then_inc(sEV, 1)
                    sc.wait_ge(sDV, d0 + 2 * tt + 1)
                    sc.activation(sb_rstd[:, tt:tt + 1], r_sc[:, 0:1], AF.Ln, bias=sb_eps[:, 0:1]).then_inc(sEV, 1)
                    sc.wait_ge(sEV, e0 + 8 * tt + 3)
                    sc.activation(sb_rstd[:, tt:tt + 1], sb_rstd[:, tt:tt + 1], AF.Exp,
                                  scale=-0.5).then_inc(sEV, 1)
                    for f in range(4):
                        _w(sc, sMM, m0, tt * 4 + f + 1)
                        dst = x1T[:, :].rearrange("p (k t) -> p k t", k=KT)[:, f * 8:(f + 1) * 8, tt * 128:(tt + 1) * 128]
                        sc.activation(dst, pT4[f % 2][:, :].rearrange("p (k t) -> p k t", k=8),
                                      AF.Copy).then_inc(sEV, 1)

            @blk.vector
            def _(v):
                d0 = C.get(sDV)
                e0 = C.get(sEV)
                m0 = C.get(sMM)
                for tt in range(8):
                    _w(v, sEV, e0, 8 * tt + 2)
                    vedge(v, v.tensor_scalar_mul(sb_mean[:, tt:tt + 1], s1c[:, 0:1], 1.0 / D_MODEL))
                    vedge(v, v.tensor_scalar_mul(r_sc[:, 1:2], s2c[:, 0:1], 1.0 / D_MODEL))
                    vedge(v, v.tensor_tensor(r_sc[:, 2:3], sb_mean[:, tt:tt + 1], sb_mean[:, tt:tt + 1], ALU.mult))
                    v.tensor_tensor(r_sc[:, 0:1], r_sc[:, 1:2], r_sc[:, 2:3], ALU.subtract).then_inc(sDV, 1)
                    _w(v, sEV, e0, 8 * tt + 4)
                    if tt >= 2:
                        _w(v, sMM, m0, 4 * (tt - 2) + 4)
                    v.tensor_scalar(x1ntl[tt % 2][:, :], x1tl[tt % 2][:, :],
                                    sb_mean[:, tt:tt + 1], sb_rstd[:, tt:tt + 1],
                                    ALU.subtract, ALU.mult).then_inc(sDV, 1)

            @blk.tensor
            def _(t):
                d0 = C.get(sDV)
                e0 = C.get(sEV)
                for tt in range(8):
                    _w(t, sDV, d0, 2 * tt + 2)
                    for f in range(4):
                        fi = tt * 4 + f
                        if fi >= 2:
                            pt, pf = (fi - 2) // 4, (fi - 2) % 4
                            _w(t, sEV, e0, 8 * pt + 5 + pf)
                        for q in range(8):
                            kk = f * 8 + q
                            mi = t.transpose(pT4[f % 2][:, q * 128:(q + 1) * 128],
                                             x1ntl[tt % 2][:, kk * 128:(kk + 1) * 128], sb_identb[:, :])
                        mi.then_inc(sMM, 1)

        C.add(sMM, 32)
        C.add(sEV, 8 * 8)
        C.add(sDV, 16)

        with nc.Block() as blk:
            @blk.tensor
            def _(t):
                m0 = C.get(sMM)
                cW4.wait(t, 0)
                x1Tv = x1T[:, :].rearrange("p (k t) -> p k t", k=KT)
                wgv = wgb[:, :].rearrange("p (k e) -> p k e", k=KT)
                for tt in range(8):
                    if tt >= 2:
                        _w(t, sEV, C.get(sEV), tt - 1)
                    for k in range(KT):
                        mi = t.matmul(pL4[tt % 2][:, 0:8], x1Tv[:, k, tt * 128:(tt + 1) * 128],
                                      wgv[:, k, :], start=(k == 0), stop=(k == KT - 1))
                    mi.then_inc(sMM, 1)

            @blk.scalar
            def _(sc):
                for tt in range(8):
                    _w(sc, sMM, C.get(sMM), tt + 1)
                    sc.activation(logtok[:, tt * 8:(tt + 1) * 8], pL4[tt % 2][:, 0:8],
                                  AF.Copy).then_inc(sEV, 1)
        C.add(sMM, 8)
        C.add(sEV, 8)

        with nc.Block() as blk:
            @blk.vector
            def _(v):
                d0 = C.get(sDV)
                e0 = C.get(sEV)
                for tt in range(8):
                    lg = logtok[:, tt * 8:(tt + 1) * 8]
                    m1 = r_m1[:, tt:tt + 1]
                    lm = r_sc[:, 0:8]
                    mi = v.reduce_max(m1, lg, AX.X)
                    vedge(v, mi)
                    v.nop().then_inc(sDV, 1)
                    vedge(v, v.tensor_scalar(lm, lg, m1, None, ALU.subtract))
                    vedge(v, v.tensor_scalar(r_sc[:, 8:16], lm, 0.0, None, ALU.is_ge))
                    vedge(v, v.scalar_tensor_tensor(r_sc[:, 16:24], r_sc[:, 8:16], -1e9, lm, ALU.mult, ALU.add))
                    mi = v.reduce_max(r_sc[:, 24:25], r_sc[:, 16:24], AX.X)
                    vedge(v, mi)
                    v.nop().then_inc(sDV, 1)
                    _w(v, sEV, e0, 2 * tt + 2)
                    vedge(v, v.tensor_scalar(r_sc[:, 8:16], lm, r_sc[:, 24:25], None, ALU.is_ge))
                    vedge(v, v.tensor_tensor(r_sc[:, 16:24], r_sc[:, 16:24], r_sc[:, 8:16], ALU.mult))
                    vedge(v, v.tensor_scalar_add(r_sc[:, 26:27], r_sc[:, 25:26], 1.0))
                    vedge(v, v.reciprocal(r_sc[:, 27:28], r_sc[:, 26:27]))
                    v.tensor_scalar_mul(route[:, tt * 8:(tt + 1) * 8], r_sc[:, 16:24],
                                        r_sc[:, 27:28]).then_inc(sDV, 1)

            @blk.scalar
            def _(sc):
                d0 = C.get(sDV)
                for tt in range(8):
                    _w(sc, sDV, d0, 3 * tt + 2)
                    sc.activation(r_sc[:, 16:24], r_sc[:, 0:8], AF.Exp).then_inc(sEV, 1)
                    sc.activation(r_sc[:, 25:26], r_sc[:, 24:25], AF.Exp).then_inc(sEV, 1)
        C.add(sDV, 24)
        C.add(sEV, 16)

        with nc.Block() as blk:
            @blk.tensor
            def _(t):
                m0 = C.get(sMM)
                cW4.wait(t, 1)
                x1Tv = x1T[:, :].rearrange("p (k t) -> p k t", k=KT)
                aTv = aTb[:, :].rearrange("p (k r) -> p k r", k=KT)
                for tt in range(8):
                    if tt >= 2:
                        _w(t, sDV, C.get(sDV), tt - 1)
                    for k in range(KT):
                        mi = t.matmul(pL4[tt % 2][:, 0:128], x1Tv[:, k, tt * 128:(tt + 1) * 128],
                                      aTv[:, k, :], start=(k == 0), stop=(k == KT - 1))
                    mi.then_inc(sMM, 1)
                    _w(t, sDV, C.get(sDV), tt + 1)
                    if tt >= 1:
                        _w(t, sEV, C.get(sEV), tt)
                    t.transpose(pT4[0][:, 0:128], hwstg[:, :], sb_identb[:, :]).then_inc(sMM, 1)

            @blk.vector
            def _(v):
                m0 = C.get(sMM)
                for tt in range(8):
                    _w(v, sMM, m0, 2 * tt + 1)
                    v.tensor_tensor(hwstg[:, :].rearrange("p (e r) -> p e r", e=8),
                                    pL4[tt % 2][:, 0:128].rearrange("p (e r) -> p e r", e=8),
                                    route[:, tt * 8:(tt + 1) * 8].unsqueeze(2).broadcast_to([128, 8, 16]),
                                    ALU.mult).then_inc(sDV, 1)

            @blk.scalar
            def _(sc):
                m0 = C.get(sMM)
                for tt in range(8):
                    _w(sc, sMM, m0, 2 * tt + 2)
                    sc.activation(hwT[:, tt * 128:(tt + 1) * 128], pT4[0][:, 0:128],
                                  AF.Copy).then_inc(sEV, 1)
        C.add(sMM, 16)
        C.add(sDV, 8)
        C.add(sEV, 8)

        _close(p4a_ctx)
        psF = [p4_ctx.enter_context(nc.psum_tensor(f"psF{i}", [128, 512], F32)) for i in range(8)]
        wb4 = [p4_ctx.enter_context(nc.sbuf_tensor(f"p4_wb{i}", [128, 16 * 512], BF16)) for i in range(2)]
        x2stg = p4_ctx.enter_context(nc.sbuf_tensor("p4_x2s", [128, 8 * 512], BF16))
        x1rld = [p4_ctx.enter_context(nc.sbuf_tensor(f"p4_rl{i}", [128, 8 * 512], BF16)) for i in range(2)]
        cW5 = Chain(nc, ctx, "cW5", 2)
        cX3 = Chain(nc, ctx, "cX3", 2)
        cX4 = Chain(nc, ctx, "cX4", 2)

        M4 = C.get(sMM)
        D4 = C.get(sDV)
        MM4, DV4 = 8, 8

        with nc.Block() as blk:
            @blk.gpsimd
            def _(g):
                for ob in range(8):
                    for ch in range(2):
                        bi = ob * 2 + ch
                        if bi >= 2:
                            _w(g, sMM, M4, ((bi - 2) // 2 + 1) * MM4)
                        cW5.issue(g, wb4[bi % 2][:, :].rearrange("p (k c) -> p k c", k=16),
                                  d_wb[ch * 2048:(ch + 1) * 2048, ob * 512:(ob + 1) * 512].rearrange("(k p) c -> p k c", p=128))

            @blk.sync
            def _(s):
                for ob in range(8):
                    if ob >= 2:
                        _w(s, sDV, D4, (ob - 1) * DV4)
                    cX3.issue(s, x1rld[ob % 2][:, :].rearrange("p (tt c) -> p tt c", tt=8),
                              x1_sp[:, ob * 512:(ob + 1) * 512].rearrange("(tt p) c -> p tt c", p=128))
                    _w(s, sDV, D4, ob * DV4 + 8)
                    cX4.issue(s, x2_sp[:, ob * 512:(ob + 1) * 512].rearrange("(tt p) c -> p tt c", p=128),
                              x2stg[:, :].rearrange("p (tt c) -> p tt c", tt=8))

            @blk.tensor
            def _(t):
                cW4.wait(t, 2)
                cW4.wait(t, 3)
                x1Tv = x1T[:, :].rearrange("p (k t) -> p k t", k=KT)
                for ob in range(8):
                    for ch in range(2):
                        bi = ob * 2 + ch
                        cW5.wait(t, bi)
                        wv = wb4[bi % 2][:, :].rearrange("p (k c) -> p k c", k=16)
                        for kk in range(16):
                            for tt in range(8):
                                if ch == 0 and kk == 0 and ob >= 1:
                                    _w(t, sDV, D4, (ob - 1) * DV4 + tt + 1)
                                t.matmul(psF[tt][:, :],
                                         x1Tv[:, ch * 16 + kk, tt * 128:(tt + 1) * 128],
                                         wv[:, kk, :],
                                         start=(ch == 0 and kk == 0), stop=False,
                                         skip_group_check=True)
                    for tt in range(8):
                        t.matmul(psF[tt][:, :], hwT[:, tt * 128:(tt + 1) * 128],
                                 bTfb[:, ob * 512:(ob + 1) * 512], start=False, stop=False,
                                 skip_group_check=True)
                        t.matmul(psF[tt][:, :], sb_ones1[:, :], sb_bbase[0:1, ob * 512:(ob + 1) * 512],
                                 start=False, stop=True, skip_group_check=True).then_inc(sMM, 1)

            @blk.vector
            def _(v):
                for ob in range(8):
                    cX3.wait(v, ob)
                    if ob >= 1:
                        cX4.wait(v, ob - 1)
                    for tt in range(8):
                        _w(v, sMM, M4, ob * MM4 + tt + 1)
                        v.tensor_scalar(x1rld[ob % 2][:, tt * 512:(tt + 1) * 512],
                                        x1rld[ob % 2][:, tt * 512:(tt + 1) * 512],
                                        sb_mean[:, tt:tt + 1], sb_rstd[:, tt:tt + 1],
                                        ALU.subtract, ALU.mult)
                        v.tensor_tensor(x2stg[:, tt * 512:(tt + 1) * 512],
                                        x1rld[ob % 2][:, tt * 512:(tt + 1) * 512],
                                        psF[tt][:, :], ALU.add).then_inc(sDV, 1)
        C.add(sMM, 8 * MM4)
        C.add(sDV, 8 * DV4)

        if "p4" in debug:
            with nc.Block() as blk:
                @blk.sync
                def _(s):
                    cH0.issue(s, dbg["x2"][:, :], x2_sp[:, :])
                    cH0.issue(s, dbg["x1T"][:, :], x1T[:, :])
                    cH0.issue(s, dbg["rt"][:, :], route[:, :])
                    cH0.issue(s, dbg["hwT"][:, :], hwT[:, :])
                    cH0.wait_last(s)

        if "p5" not in phases:
            raise _StopBuild()

        _close(p4_ctx)
        p5_ctx = _mkstack(_cl)
        x2tl = [p5_ctx.enter_context(nc.sbuf_tensor(f"p5_x2{i}", [128, D_MODEL], BF16)) for i in range(2)]
        ostg = [p5_ctx.enter_context(nc.sbuf_tensor(f"p5_o{i}", [128, D_MODEL], F32)) for i in range(2)]
        dump5 = p5_ctx.enter_context(nc.sbuf_tensor("p5_dump", [128, D_MODEL], BF16))
        s1c = p5_ctx.enter_context(nc.sbuf_tensor("p5_s1", [128, 1], F32))
        s2c = p5_ctx.enter_context(nc.sbuf_tensor("p5_s2", [128, 1], F32))
        cX5 = Chain(nc, ctx, "cX5", 2)
        cOUT = Chain(nc, ctx, "cOUT", 2)

        with nc.Block() as blk:
            @blk.sync
            def _(s):
                for tt in range(8):
                    if tt >= 2:
                        _w(s, sDV, C.get(sDV), 2 * (tt - 2) + 2)
                    cX5.issue(s, x2tl[tt % 2][:, :], x2_sp[tt * 128:(tt + 1) * 128, :])
                    _w(s, sDV, C.get(sDV), 2 * tt + 2)
                    cOUT.issue(s, d_out[tt * 128:(tt + 1) * 128, :], ostg[tt % 2][:, :])
                cOUT.wait_last(s)

            @blk.scalar
            def _(sc):
                e0 = C.get(sEV)
                for tt in range(8):
                    cX5.wait(sc, tt)
                    if tt >= 1:
                        sc.wait_ge(sDV, C.get(sDV) + 2 * (tt - 1) + 1)
                    sc.activation(dump5[:, :], x2tl[tt % 2][:, :], AF.Copy,
                                  accum_out=s1c[:, 0:1]).then_inc(sEV, 1)
                    sc.wait_ge(sEV, e0 + 4 * tt + 1)
                    sc.activation(dump5[:, :], x2tl[tt % 2][:, :], AF.Square,
                                  accum_out=s2c[:, 0:1]).then_inc(sEV, 1)
                    sc.wait_ge(sDV, C.get(sDV) + 2 * tt + 1)
                    sc.activation(sb_mean2[:, 1:2], sb_mean2[:, 2:3], AF.Ln, bias=sb_eps[:, 0:1]).then_inc(sEV, 1)
                    sc.wait_ge(sEV, e0 + 4 * tt + 3)
                    sc.activation(sb_mean2[:, 1:2], sb_mean2[:, 1:2], AF.Exp, scale=-0.5).then_inc(sEV, 1)

            @blk.vector
            def _(v):
                d0 = C.get(sDV)
                e0 = C.get(sEV)
                for tt in range(8):
                    _w(v, sEV, e0, 4 * tt + 2)
                    vedge(v, v.tensor_scalar_mul(sb_mean2[:, 0:1], s1c[:, 0:1], 1.0 / D_MODEL))
                    vedge(v, v.tensor_scalar_mul(sb_mean2[:, 3:4], s2c[:, 0:1], 1.0 / D_MODEL))
                    vedge(v, v.tensor_tensor(sb_mean2[:, 4:5], sb_mean2[:, 0:1], sb_mean2[:, 0:1], ALU.mult))
                    v.tensor_tensor(sb_mean2[:, 2:3], sb_mean2[:, 3:4], sb_mean2[:, 4:5],
                                    ALU.subtract).then_inc(sDV, 1)
                    _w(v, sEV, e0, 4 * tt + 4)
                    if tt >= 2:
                        cOUT.wait(v, tt - 2)
                    v.tensor_scalar(ostg[tt % 2][:, :], x2tl[tt % 2][:, :],
                                    sb_mean2[:, 0:1], sb_mean2[:, 1:2],
                                    ALU.subtract, ALU.mult).then_inc(sDV, 1)
        C.add(sDV, 16)
        C.add(sEV, 32)

      except _StopBuild:
        pass
      for st in reversed(open_stacks):
          st.close()
      open_stacks.clear()
    nc.finalize()
    return nc, dbg



def _ktile_major(a):
    k128, n = a.shape
    k = k128 // 128
    return np.ascontiguousarray(a.reshape(k, 128, n).transpose(1, 0, 2).reshape(128, k * n))


def make_inputs(full):
    hs = np.asarray(full["L0_attn_hidden_states"], dtype=np.float32)
    wip = np.asarray(full["L0_attn_in_proj_w"], np.float32)
    convw = np.asarray(full["L0_attn_conv_w"], np.float32)
    convb = np.asarray(full["L0_attn_conv_b"], np.float32)
    dtb = np.asarray(full["L0_attn_dt_bias"], np.float32)
    alog = np.asarray(full["L0_attn_A_log"], np.float32)
    dvec = np.asarray(full["L0_attn_D"], np.float32)
    nrmw = np.asarray(full["L0_attn_norm_w"], np.float32)
    wo = np.asarray(full["L0_attn_out_proj_w"], np.float32)
    wb = np.asarray(full["L0_ffn_W_base"], np.float32)
    bb = np.asarray(full["L0_ffn_b_base"], np.float32)
    wg = np.asarray(full["L0_ffn_W_gate"], np.float32)
    A = np.asarray(full["L0_ffn_A"], np.float32)
    B = np.asarray(full["L0_ffn_B"], np.float32)

    shared = {
        "wipT": np.ascontiguousarray(wip.T),
        "woT": np.ascontiguousarray(wo.T),
        "wbT": np.ascontiguousarray(wb.T),
        "wgT": _ktile_major(np.ascontiguousarray(wg.T)),
        "aT": _ktile_major(np.ascontiguousarray(A.reshape(128, 4096).T)),
        "bTf": np.ascontiguousarray(B.transpose(0, 2, 1).reshape(128, 4096)),
        "convw": np.ascontiguousarray(convw.reshape(80, 128, 4).transpose(1, 0, 2).reshape(128, 320)),
        "convb": np.ascontiguousarray(convb.reshape(80, 128).T),
        "dtb": np.ascontiguousarray(dtb.reshape(128, 1)),
        "alog": np.ascontiguousarray(alog.reshape(128, 1)),
        "drow": np.ascontiguousarray(dvec.reshape(1, 128)),
        "nrmw": np.ascontiguousarray(nrmw.reshape(64, 128).T),
        "bbase": np.ascontiguousarray(bb.reshape(1, 4096)),
        "maskut": np.triu(np.ones((128, 128), np.float32)),
        "ident": np.eye(128, dtype=np.float32),
    }
    in_maps = []
    for c in range(8):
        b, h = c // 2, c % 2
        shard = hs[b, h * TOK:(h + 1) * TOK, :]
        role = np.zeros((128, 2), np.float32)
        role[:, 0] = 1.0 if h == 0 else 0.0
        role[:, 1] = 1.0 if h == 1 else 0.0
        m = dict(shared)
        m["xT"] = _ktile_major(np.ascontiguousarray(shard.T))
        m["xres"] = np.ascontiguousarray(shard)
        m["role"] = role
        in_maps.append(m)
    return in_maps


_CACHE = {}


def kernel(**inputs):
    if "nc" not in _CACHE:
        nc, _ = build_program()
        _CACHE["nc"] = nc
    nc = _CACHE["nc"]
    in_maps = make_inputs(inputs)
    res = run_bass_kernel_spmd(nc, in_maps, core_ids=list(range(8)))
    out = np.empty((4, 2048, 4096), np.float32)
    for c in range(8):
        b, h = c // 2, c % 2
        out[b, h * TOK:(h + 1) * TOK, :] = res.results[c]["out"]
    return out


# version 04 (final)
# speedup vs baseline: 1.1694x; 1.1694x over previous
import numpy as np
import ml_dtypes
import concourse.bass as bass
import concourse.mybir as mybir
from concourse.bass_utils import run_bass_kernel_spmd

F32 = mybir.dt.float32
BF16 = mybir.dt.bfloat16
AF = mybir.ActivationFunctionType
ALU = mybir.AluOpType
AX = mybir.AxisListType

D_MODEL = 4096
D_INNER = 8192
NH = 128
HD = 64
NG = 8
DS = 128
OC = 18560
Z_OC = 8192
XBC_OC = 10240
TOK = 1024
CH = 128
NCH = TOK // CH
KT = D_MODEL // 128
EPS = 1e-5
ALL_PHASES = ("p1", "p19", "p15", "p2", "p29", "p2c", "p3", "p4", "p5")


class Ctr:
    def __init__(self):
        self.v = {}

    def add(self, name, n):
        self.v[name] = self.v.get(name, 0) + n
        return self.v[name]

    def get(self, name):
        return self.v.get(name, 0)


class Chain:

    def __init__(self, nc, ctx, name, n=2):
        self.sems = [ctx.enter_context(nc.semaphore(f"{name}{i}")) for i in range(n)]
        self.n = n
        self.count = [0] * n
        self.marks = []

    def issue(self, eng, out, in_):
        j = len(self.marks) % self.n
        ins = eng.dma_start(out, in_).then_inc(self.sems[j], 16)
        self.count[j] += 1
        self.marks.append((j, 16 * self.count[j]))
        return ins

    def wait(self, eng, idx):
        j = idx % self.n
        eng.wait_ge(self.sems[j], 16 * (idx // self.n + 1))

    def wait_last(self, eng):
        for j in range(self.n):
            if self.count[j]:
                eng.wait_ge(self.sems[j], 16 * self.count[j])


def build_program(debug=(), phases=ALL_PHASES):
    nc = bass.Bass("TRN2", target_bir_lowering=False)

    d_xT = nc.dram_tensor("xT", [128, KT * TOK], F32, kind="ExternalInput")
    d_xres = nc.dram_tensor("xres", [TOK, D_MODEL], F32, kind="ExternalInput")
    d_wip = nc.dram_tensor("wipT", [D_MODEL, OC], F32, kind="ExternalInput")
    d_wo = nc.dram_tensor("woT", [D_INNER, D_MODEL], F32, kind="ExternalInput")
    d_wb = nc.dram_tensor("wbT", [D_MODEL, D_MODEL], F32, kind="ExternalInput")
    d_wg = nc.dram_tensor("wgT", [128, KT * 8], F32, kind="ExternalInput")
    d_aT = nc.dram_tensor("aT", [128, KT * 128], F32, kind="ExternalInput")
    d_bTf = nc.dram_tensor("bTf", [128, D_MODEL], F32, kind="ExternalInput")
    d_convw = nc.dram_tensor("convw", [128, 80 * 4], F32, kind="ExternalInput")
    d_convb = nc.dram_tensor("convb", [128, 80], F32, kind="ExternalInput")
    d_dtb = nc.dram_tensor("dtb", [128, 1], F32, kind="ExternalInput")
    d_alog = nc.dram_tensor("alog", [128, 1], F32, kind="ExternalInput")
    d_drow = nc.dram_tensor("drow", [1, 128], F32, kind="ExternalInput")
    d_nrmw = nc.dram_tensor("nrmw", [128, 64], F32, kind="ExternalInput")
    d_bbase = nc.dram_tensor("bbase", [1, D_MODEL], F32, kind="ExternalInput")
    d_mask = nc.dram_tensor("maskut", [128, 128], F32, kind="ExternalInput")
    d_id = nc.dram_tensor("ident", [128, 128], F32, kind="ExternalInput")
    d_role = nc.dram_tensor("role", [128, 2], F32, kind="ExternalInput")

    d_out = nc.dram_tensor("out", [TOK, D_MODEL], F32, kind="ExternalOutput")

    z_sp = nc.dram_tensor("z_sp", [TOK, D_INNER], BF16)
    xr_sp = nc.dram_tensor("xr_sp", [XBC_OC, TOK], BF16)
    xc_sp = nc.dram_tensor("xc_sp", [XBC_OC, TOK], BF16)
    yr_sp = nc.dram_tensor("yr_sp", [TOK, D_INNER], BF16)
    yT_sp = nc.dram_tensor("yT_sp", [D_INNER, TOK], BF16)
    x1_sp = nc.dram_tensor("x1_sp", [TOK, D_MODEL], BF16)
    x2_sp = nc.dram_tensor("x2_sp", [TOK, D_MODEL], BF16)
    h_in = nc.dram_tensor("h_in", [128, 240], BF16)
    h_out = nc.dram_tensor("h_out", [128, 240], BF16)
    st_in = nc.dram_tensor("st_in", [128, D_INNER], BF16)
    st_out = nc.dram_tensor("st_out", [128, D_INNER], BF16)

    dbg = {}
    if "p1" in debug:
        dbg["z"] = nc.dram_tensor("dbg_z", [TOK, D_INNER], BF16, kind="ExternalOutput")
        dbg["xr"] = nc.dram_tensor("dbg_xr", [XBC_OC, TOK], BF16, kind="ExternalOutput")
        dbg["xc"] = nc.dram_tensor("dbg_xc", [XBC_OC, TOK], BF16, kind="ExternalOutput")
        dbg["dt"] = nc.dram_tensor("dbg_dt", [128, TOK], F32, kind="ExternalOutput")
    if "p2" in debug:
        dbg["yr"] = nc.dram_tensor("dbg_yr", [TOK, D_INNER], BF16, kind="ExternalOutput")
        dbg["st"] = nc.dram_tensor("dbg_st", [128, D_INNER], BF16, kind="ExternalOutput")
    if "p2c" in debug:
        dbg["yT"] = nc.dram_tensor("dbg_yT", [D_INNER, TOK], BF16, kind="ExternalOutput")
    if "p3" in debug:
        dbg["x1"] = nc.dram_tensor("dbg_x1", [TOK, D_MODEL], BF16, kind="ExternalOutput")
    if "p4" in debug:
        dbg["x2"] = nc.dram_tensor("dbg_x2", [TOK, D_MODEL], BF16, kind="ExternalOutput")
        dbg["x1T"] = nc.dram_tensor("dbg_x1T", [128, KT * TOK], BF16, kind="ExternalOutput")
        dbg["rt"] = nc.dram_tensor("dbg_rt", [128, 64], BF16, kind="ExternalOutput")
        dbg["hwT"] = nc.dram_tensor("dbg_hwT", [128, TOK], BF16, kind="ExternalOutput")

    C = Ctr()

    class _StopBuild(Exception):
        pass

    import contextlib
    ctx = contextlib.ExitStack()
    open_stacks = []

    def _mkstack(cl):
        st = cl.ExitStack()
        open_stacks.append(st)
        return st

    def _close(st):
        assert open_stacks and open_stacks[-1] is st
        open_stacks.pop()
        st.close()

    with ctx:
      try:
        sMM = ctx.enter_context(nc.semaphore("sMM"))
        sEV = ctx.enter_context(nc.semaphore("sEV"))
        sDV = ctx.enter_context(nc.semaphore("sDV"))
        sCC = ctx.enter_context(nc.semaphore("sCC"))
        sVS = ctx.enter_context(nc.semaphore("sVS"))
        vs_n = [0]

        def vedge(v, ins):
            vs_n[0] += 1
            ins.then_inc(sVS, 1)
            v.wait_ge(sVS, vs_n[0])
        cG0 = Chain(nc, ctx, "cG0", 2)
        cW = Chain(nc, ctx, "cW", 2)
        cSPz = Chain(nc, ctx, "cSPz", 2)
        cH0 = Chain(nc, ctx, "cH0", 2)
        cCV = Chain(nc, ctx, "cCV", 2)
        cCS = Chain(nc, ctx, "cCS", 2)
        cXC = Chain(nc, ctx, "cXC", 2)

        sb_convw = ctx.enter_context(nc.sbuf_tensor("c_convw", [128, 320], F32))
        sb_convb = ctx.enter_context(nc.sbuf_tensor("c_convb", [128, 80], F32))
        sb_dtb = ctx.enter_context(nc.sbuf_tensor("c_dtb", [128, 1], F32))
        sb_A = ctx.enter_context(nc.sbuf_tensor("c_A", [128, 1], F32))
        sb_nrmw = ctx.enter_context(nc.sbuf_tensor("c_nrmw", [128, 64], F32))
        sb_maskb = ctx.enter_context(nc.sbuf_tensor("c_maskb", [128, 128], BF16))
        sb_identb = ctx.enter_context(nc.sbuf_tensor("c_identb", [128, 128], BF16))
        sb_ones1 = ctx.enter_context(nc.sbuf_tensor("c_ones1", [1, 128], BF16))
        sb_role = ctx.enter_context(nc.sbuf_tensor("c_role", [128, 2], F32))
        sb_Db = ctx.enter_context(nc.sbuf_tensor("c_Db", [128, 128], BF16))
        sb_drow = ctx.enter_context(nc.sbuf_tensor("c_drow", [1, 128], BF16))
        sb_halo = ctx.enter_context(nc.sbuf_tensor("c_halo", [128, 240], BF16))
        sb_halou = ctx.enter_context(nc.sbuf_tensor("c_halou", [128, 240], BF16))
        sb_rs = ctx.enter_context(nc.sbuf_tensor("c_rs", [128, NCH], F32))
        sb_sig = ctx.enter_context(nc.sbuf_tensor("c_sig", [128, NCH + 1], F32))
        sb_mean = ctx.enter_context(nc.sbuf_tensor("c_mean", [128, 8], F32))
        sb_rstd = ctx.enter_context(nc.sbuf_tensor("c_rstd", [128, 8], F32))
        sb_mean2 = ctx.enter_context(nc.sbuf_tensor("c_mean2", [128, 8], F32))
        sb_eps = ctx.enter_context(nc.sbuf_tensor("c_eps", [128, 1], F32))
        sb_rstd2 = ctx.enter_context(nc.sbuf_tensor("c_rstd2", [128, 8], F32))

        import contextlib as _cl
        s_ctx = _mkstack(_cl)
        S_sb = s_ctx.enter_context(nc.sbuf_tensor("p2_S", [128, D_INNER], BF16))
        sb_dtraw = s_ctx.enter_context(nc.sbuf_tensor("c_dtraw", [128, TOK], F32))
        sb_dt = s_ctx.enter_context(nc.sbuf_tensor("c_dt", [128, TOK], F32))
        sb_cs = s_ctx.enter_context(nc.sbuf_tensor("c_cs", [128, TOK], F32))
        p1_ctx = _mkstack(_cl)
        ps = [p1_ctx.enter_context(nc.psum_tensor(f"ps{i}", [128, 512], F32)) for i in range(8)]
        with nc.Block() as blk:
            @blk.sync
            def _(s):
                cH0.issue(s, sb_convw[:, :], d_convw[:, :])
                cH0.issue(s, sb_convb[:, :], d_convb[:, :])
                cH0.issue(s, sb_dtb[:, :], d_dtb[:, :])
                cH0.issue(s, sb_A[:, :], d_alog[:, :])
                cH0.issue(s, sb_nrmw[:, :], d_nrmw[:, :])
                cH0.issue(s, sb_role[:, :], d_role[:, :])

            @blk.gpsimd
            def _(g):
                cG0.issue(g, sb_maskb[:, :], d_mask[:, :])
                cG0.issue(g, sb_identb[:, :], d_id[:, :])
                cG0.issue(g, sb_drow[:, :], d_drow[:, :])

            @blk.vector
            def _(v):
                cG0.wait_last(v)
                cH0.wait_last(v)
                v.memset(sb_eps[:, :], EPS)
                v.memset(sb_ones1[:, :], 1.0).then_inc(sDV, 1)

            @blk.scalar
            def _(sc):
                cH0.wait_last(sc)
                sc.activation(sb_cs[:, 0:1], sb_A[:, :], AF.Exp).then_inc(sEV, 1)

            @blk.tensor
            def _(t):
                cG0.wait_last(t)
                t.wait_ge(sDV, C.get(sDV) + 1)
                t.matmul(ps[7][:, 0:128], sb_ones1[:, :], sb_drow[:, :],
                         start=True, stop=True).then_inc(sMM, 1)
        C.add(sEV, 1)
        C.add(sMM, 1)
        C.add(sDV, 1)

        with nc.Block() as blk:
            @blk.vector
            def _(v):
                v.tensor_scalar_mul(sb_A[:, :], sb_cs[:, 0:1], -1.0)
                v.tensor_copy(sb_Db[:, :], ps[7][:, 0:128])

        if "p1" not in phases:
            raise _StopBuild()

        chunks = [("z", i) for i in range(16)] + [("x", i) for i in range(20)] + [("dt", 0)]
        nchunks = len(chunks)
        banks_per_chunk = [8] * 36 + [2]

        def cum_banks(i):
            return sum(banks_per_chunk[:i])

        sb_xTb = p1_ctx.enter_context(nc.sbuf_tensor("p1_xTb", [128, KT * TOK], BF16))
        wbufs = [p1_ctx.enter_context(nc.sbuf_tensor(f"p1_w{i}", [128, KT * 512], BF16)) for i in range(2)]
        zstg = [p1_ctx.enter_context(nc.sbuf_tensor(f"p1_zs{i}", [128, 8 * 512], BF16)) for i in range(2)]

        spill_idx = {}
        _nsp = 0
        for i, (kind, idx) in enumerate(chunks):
            if kind != "dt":
                spill_idx[i] = _nsp
                _nsp += 1

        with nc.Block() as blk:
            @blk.gpsimd
            def _(g):
                xt_issue = len(cG0.marks)
                cG0.issue(g, sb_xTb[:, :], d_xT[:, :])
                for i, (kind, idx) in enumerate(chunks):
                    if kind == "z":
                        c0, w = idx * 512, 512
                    elif kind == "x":
                        c0, w = Z_OC + idx * 512, 512
                    else:
                        c0, w = Z_OC + XBC_OC, 128
                    if i >= 2:
                        g.wait_ge(sMM, C.get(sMM) + cum_banks(i - 1))
                    cW.issue(g,
                             wbufs[i % 2][:, 0:KT * w].rearrange("p (k c) -> p k c", k=KT),
                             d_wip[:, c0:c0 + w].rearrange("(k p) c -> p k c", p=128))

            @blk.tensor
            def _(t):
                cG0.wait(t, 3)
                for i, (kind, idx) in enumerate(chunks):
                    cW.wait(t, i)
                    wb = wbufs[i % 2][:, :].rearrange("p (k c) -> p k c", k=KT)
                    if kind == "z":
                        for tt in range(8):
                            if i >= 1:
                                t.wait_ge(sEV, C.get(sEV) + cum_banks(i - 1) + tt + 1)
                            for k in range(KT):
                                mi = t.matmul(ps[tt][:, :],
                                              sb_xTb[:, k * TOK + tt * 128: k * TOK + (tt + 1) * 128],
                                              wb[:, k, 0:512],
                                              start=(k == 0), stop=(k == KT - 1))
                            mi.then_inc(sMM, 1)
                    elif kind == "x":
                        for oct_ in range(4):
                            for t2 in range(2):
                                bi = oct_ * 2 + t2
                                if i >= 1:
                                    t.wait_ge(sEV, C.get(sEV) + cum_banks(i - 1) + bi + 1)
                                for k in range(KT):
                                    mi = t.matmul(ps[bi][:, :],
                                                  wb[:, k, oct_ * 128:(oct_ + 1) * 128],
                                                  sb_xTb[:, k * TOK + t2 * 512: k * TOK + (t2 + 1) * 512],
                                                  start=(k == 0), stop=(k == KT - 1))
                                mi.then_inc(sMM, 1)
                    else:
                        wbd = wbufs[i % 2][:, 0:KT * 128].rearrange("p (k c) -> p k c", k=KT)
                        for t2 in range(2):
                            t.wait_ge(sEV, C.get(sEV) + cum_banks(i - 1) + t2 + 1)
                            for k in range(KT):
                                mi = t.matmul(ps[t2][:, :],
                                              wbd[:, k, 0:128],
                                              sb_xTb[:, k * TOK + t2 * 512: k * TOK + (t2 + 1) * 512],
                                              start=(k == 0), stop=(k == KT - 1))
                            mi.then_inc(sMM, 1)

            @blk.scalar
            def _(sc):
                for i, (kind, idx) in enumerate(chunks):
                    stg = zstg[i % 2]
                    for bi in range(banks_per_chunk[i]):
                        sc.wait_ge(sMM, C.get(sMM) + cum_banks(i) + bi + 1)
                        if i >= 2 and bi == 0 and (i - 2) in spill_idx:
                            cSPz.wait(sc, spill_idx[i - 2])
                            if 16 <= i - 2 < 36:
                                sc.wait_ge(sDV, C.get(sDV) + (i - 2 - 16 + 1))
                        if kind == "z":
                            sc.activation(stg[:, bi * 512:(bi + 1) * 512], ps[bi][:, :],
                                          AF.Silu).then_inc(sEV, 1)
                        elif kind == "x":
                            sc.activation(stg[:, bi * 512:(bi + 1) * 512], ps[bi][:, :],
                                          AF.Copy).then_inc(sEV, 1)
                        else:
                            sc.activation(sb_dtraw[:, bi * 512:(bi + 1) * 512], ps[bi][:, :],
                                          AF.Copy).then_inc(sEV, 1)

            @blk.sync
            def _(s):
                for i, (kind, idx) in enumerate(chunks):
                    if kind == "dt":
                        continue
                    s.wait_ge(sEV, C.get(sEV) + cum_banks(i) + banks_per_chunk[i])
                    stg = zstg[i % 2]
                    if kind == "z":
                        cSPz.issue(s,
                                   z_sp[:, idx * 512:(idx + 1) * 512].rearrange("(tt p) c -> p tt c", p=128),
                                   stg[:, :].rearrange("p (tt c) -> p tt c", tt=8))
                    else:
                        cSPz.issue(s,
                                   xr_sp[idx * 512:(idx + 1) * 512, :].rearrange("(o p) t -> p o t", p=128),
                                   stg[:, :].rearrange("p (o t) -> p o t", o=4))

            @blk.vector
            def _(v):
                for i, (kind, idx) in enumerate(chunks):
                    if kind != "x":
                        continue
                    stg = zstg[i % 2]
                    for oct_ in range(4):
                        v.wait_ge(sEV, C.get(sEV) + cum_banks(i) + oct_ * 2 + 2)
                        g_tile = idx * 4 + oct_
                        ci = v.tensor_copy(sb_halo[:, g_tile * 3:(g_tile + 1) * 3],
                                           stg[:, (oct_ * 2 + 1) * 512 + 509: (oct_ * 2 + 1) * 512 + 512])
                    ci.then_inc(sDV, 1)
                cSPz.wait_last(v)

        C.add(sMM, cum_banks(nchunks))
        C.add(sEV, cum_banks(nchunks))
        C.add(sDV, 20)

        if "p1" in debug:
            with nc.Block() as blk:
                @blk.sync
                def _(s):
                    cH0.issue(s, dbg["z"][:, :], z_sp[:, :])
                    cH0.wait(s, len(cH0.marks) - 1)
                    cH0.issue(s, dbg["xr"][:, :], xr_sp[:, :])
                    cH0.wait(s, len(cH0.marks) - 1)
                    cH0.issue(s, dbg["dt"][:, :], sb_dtraw[:, :])
                    cH0.wait_last(s)

        if "p19" not in phases:
            raise _StopBuild()

        with nc.named_scope("P19"), nc.Block() as blk:
            hin_issue = len(cXC.marks)

            @blk.vector
            def _(v):
                v.tensor_scalar_mul(sb_halo[:, :], sb_halo[:, :], sb_role[:, 0:1]).then_inc(sDV, 1)
                cXC.wait(v, hin_issue + 1)
                v.tensor_scalar_mul(sb_halou[:, :], sb_halou[:, :], sb_role[:, 1:2]).then_inc(sDV, 1)

            @blk.sync
            def _(s):
                s.wait_ge(sDV, C.get(sDV) + 1)
                cXC.issue(s, h_in[:, :], sb_halo[:, :])

            @blk.gpsimd
            def _(g):
                cXC.wait(g, hin_issue)
                g.collective_compute("AllReduce", ALU.add,
                                     replica_groups=[[0, 1], [2, 3], [4, 5], [6, 7]],
                                     ins=[h_in.ap().opt()], outs=[h_out.ap().opt()]).then_inc(sCC, 1)

            @blk.scalar
            def _(sc):
                sc.wait_ge(sCC, C.get(sCC) + 1)
                cXC.issue(sc, sb_halou[:, :], h_out[:, :])
        C.add(sDV, 2)
        C.add(sCC, 1)

        if "p15" not in phases:
            raise _StopBuild()

        cin = [p1_ctx.enter_context(nc.sbuf_tensor(f"cv_in{i}", [128, 1027], BF16)) for i in range(2)]
        cdg = [p1_ctx.enter_context(nc.sbuf_tensor(f"cv_dg{i}", [128, 4 * 128], BF16)) for i in range(2)]
        cst = [p1_ctx.enter_context(nc.sbuf_tensor(f"cv_st{i}", [128, 1024], BF16)) for i in range(2)]
        NT = 80

        with nc.named_scope("P15"), nc.Block() as blk:
            @blk.sync
            def _(s):
                for o in range(NT):
                    if o >= 2:
                        s.wait_ge(sMM, C.get(sMM) + 2 * (o - 1))
                    cCV.issue(s, cin[o % 2][:, 3:1027], xr_sp[o * 128:(o + 1) * 128, :])

            @blk.vector
            def _(v):
                for o in range(NT):
                    cCV.wait(v, o)
                    if o >= 2:
                        v.wait_ge(sMM, C.get(sMM) + 2 * (o - 1))
                    v.tensor_copy(cin[o % 2][:, 0:3], sb_halou[:, o * 3:(o + 1) * 3])
                    for k in range(4):
                        di = v.tensor_scalar_mul(cdg[o % 2][:, k * 128:(k + 1) * 128],
                                                 sb_identb[:, :], sb_convw[:, o * 4 + k: o * 4 + k + 1])
                    di.then_inc(sDV, 1)

            @blk.tensor
            def _(t):
                for o in range(NT):
                    t.wait_ge(sDV, C.get(sDV) + o + 1)
                    for t2 in range(2):
                        if o >= 1:
                            t.wait_ge(sEV, C.get(sEV) + 2 * (o - 1) + t2 + 1)
                        for k in range(4):
                            mi = t.matmul(ps[t2][:, :],
                                          cdg[o % 2][:, k * 128:(k + 1) * 128],
                                          cin[o % 2][:, t2 * 512 + k: t2 * 512 + k + 512],
                                          start=(k == 0), stop=(k == 3))
                        mi.then_inc(sMM, 1)

            @blk.scalar
            def _(sc):
                for o in range(NT):
                    for t2 in range(2):
                        sc.wait_ge(sMM, C.get(sMM) + 2 * o + t2 + 1)
                        if o >= 2 and t2 == 0:
                            cCS.wait(sc, o - 2)
                        sc.activation(cst[o % 2][:, t2 * 512:(t2 + 1) * 512], ps[t2][:, :],
                                      AF.Silu, bias=sb_convb[:, o:o + 1]).then_inc(sEV, 1)

            @blk.gpsimd
            def _(g):
                for o in range(NT):
                    g.wait_ge(sEV, C.get(sEV) + 2 * o + 2)
                    cCS.issue(g, xc_sp[o * 128:(o + 1) * 128, :], cst[o % 2][:, :])
                cCS.wait_last(g)

        C.add(sDV, NT)
        C.add(sMM, 2 * NT)
        C.add(sEV, 2 * NT)

        if "p1" in debug:
            with nc.Block() as blk:
                @blk.sync
                def _(s):
                    cH0.issue(s, dbg["xc"][:, :], xc_sp[:, :])
                    cH0.wait_last(s)

        if "p2" not in phases:
            raise _StopBuild()

        _close(p1_ctx)

        p2_ctx = _mkstack(_cl)
        psT = [p2_ctx.enter_context(nc.psum_tensor(f"psT{i}", [128, 1024], BF16)) for i in range(2)]
        psD = [p2_ctx.enter_context(nc.psum_tensor(f"psD{i}", [128, 512], F32)) for i in range(2)]
        psY = [p2_ctx.enter_context(nc.psum_tensor(f"psY{i}", [128, 512], F32)) for i in range(2)]
        psS = p2_ctx.enter_context(nc.psum_tensor("psS", [128, 512], F32))
        psE = p2_ctx.enter_context(nc.psum_tensor("psE", [128, 512], F32))

        sb_xch1 = p2_ctx.enter_context(nc.sbuf_tensor("p2_x0", [128, 64 * 128], BF16))
        sb_xch = [sb_xch1, sb_xch1]
        sb_bcb = [p2_ctx.enter_context(nc.sbuf_tensor(f"p2_bc{i}", [128, 16 * 128], BF16)) for i in range(2)]
        x_tok = p2_ctx.enter_context(nc.sbuf_tensor("p2_xtok", [128, D_INNER], BF16))
        xdt = p2_ctx.enter_context(nc.sbuf_tensor("p2_xdt", [128, D_INNER], BF16))
        xD = p2_ctx.enter_context(nc.sbuf_tensor("p2_xD", [128, D_INNER], BF16))
        ystage = p2_ctx.enter_context(nc.sbuf_tensor("p2_yst", [128, D_INNER], BF16))
        B_tok = p2_ctx.enter_context(nc.sbuf_tensor("p2_btok", [128, 8 * 128], BF16))
        sb_Gm = p2_ctx.enter_context(nc.sbuf_tensor("p2_gm", [128, 8 * 128], BF16))
        FLn = p2_ctx.enter_context(nc.sbuf_tensor("p2_fln", [2, 16384], BF16))
        FLc = p2_ctx.enter_context(nc.sbuf_tensor("p2_flc", [2, 16384], BF16))
        sb_ones2 = p2_ctx.enter_context(nc.sbuf_tensor("p2_ones2", [2, 128], BF16))
        dtT_sb = p2_ctx.enter_context(nc.sbuf_tensor("p2_dtT", [128, 128], BF16))
        dsT_sb = p2_ctx.enter_context(nc.sbuf_tensor("p2_dsT", [128, 128], BF16))
        odT_sb = p2_ctx.enter_context(nc.sbuf_tensor("p2_odT", [128, 128], BF16))
        e_row = p2_ctx.enter_context(nc.sbuf_tensor("p2_erow", [1, 128], BF16))
        E_bc = p2_ctx.enter_context(nc.sbuf_tensor("p2_Ebc", [128, 128], BF16))
        sb_dsb = p2_ctx.enter_context(nc.sbuf_tensor("p2_dsb", [128, 128], BF16))
        sb_odb = p2_ctx.enter_context(nc.sbuf_tensor("p2_odb", [128, 128], BF16))
        sb_ecolb = p2_ctx.enter_context(nc.sbuf_tensor("p2_ecol", [128, 1], BF16))
        sb_dtb16 = p2_ctx.enter_context(nc.sbuf_tensor("p2_dtb16", [128, 128], BF16))
        sb_csh = p2_ctx.enter_context(nc.sbuf_tensor("p2_csh", [128, 128], BF16))
        sb_csl = p2_ctx.enter_context(nc.sbuf_tensor("p2_csl", [128, 128], BF16))
        sb_ncsh = p2_ctx.enter_context(nc.sbuf_tensor("p2_ncsh", [128, 128], BF16))
        sb_ncsl = p2_ctx.enter_context(nc.sbuf_tensor("p2_ncsl", [128, 128], BF16))
        Fb = [p2_ctx.enter_context(nc.sbuf_tensor(f"p2_F{i}", [128, 512], BF16)) for i in range(4)]
        Rb = [p2_ctx.enter_context(nc.sbuf_tensor(f"p2_R{i}", [128, 512], BF16)) for i in range(4)]
        sb_yo = [p2_ctx.enter_context(nc.sbuf_tensor(f"p2_yo{i}", [128, 512], BF16)) for i in range(2)]
        sb_sc = [p2_ctx.enter_context(nc.sbuf_tensor(f"p2_sc{i}", [128, 512], BF16)) for i in range(2)]
        sb_t2 = p2_ctx.enter_context(nc.sbuf_tensor("p2_t2", [128, 512], BF16))
        sb_t3 = sb_t2
        sb_xds = [p2_ctx.enter_context(nc.sbuf_tensor(f"p2_xds{i}", [128, 512], BF16)) for i in range(2)]
        sb_dA = sb_dtraw
        sb_tmpA = sb_cs
        sb_tmpB = sb_dtraw

        cL = Chain(nc, ctx, "cL", 2)
        cFL = Chain(nc, ctx, "cFL", 4)
        cYR = Chain(nc, ctx, "cYR", 2)

        with nc.named_scope("P2prep"), nc.Block() as blk:
            @blk.scalar
            def _(sc):
                sc.wait_ge(sDV, C.get(sDV) + 1)
                ev = C.get(sEV)
                sc.activation(sb_tmpA[:, :], sb_dt[:, :], AF.Abs).then_inc(sEV, 1)
                sc.wait_ge(sEV, ev + 1)
                sc.activation(sb_tmpA[:, 0:512], sb_tmpA[:, 0:512], AF.Exp, scale=-1.0).then_inc(sEV, 1)
                sc.activation(sb_tmpA[:, 512:1024], sb_tmpA[:, 512:1024], AF.Exp, scale=-1.0).then_inc(sEV, 1)
                sc.wait_ge(sEV, ev + 3)
                sc.activation(sb_tmpA[:, 0:512], sb_tmpA[:, 0:512], AF.Ln, bias=1.0).then_inc(sEV, 1)
                sc.activation(sb_tmpA[:, 512:1024], sb_tmpA[:, 512:1024], AF.Ln, bias=1.0).then_inc(sEV, 1)

            @blk.vector
            def _(v):
                v.tensor_scalar_add(sb_dt[:, :], sb_dtraw[:, :], sb_dtb[:, 0:1]).then_inc(sDV, 1)

                v.memset(sb_ones2[:, :], 1.0)
                v.memset(S_sb[:, :], 0.0)
                v.memset(psE[:, 0:128], 1.0)
                v.memset(sb_sig[:, :], 0.0)
                v.wait_ge(sEV, C.get(sEV) + 1)
                v.tensor_scalar_max(sb_tmpB[:, :], sb_dt[:, :], 0.0)
                v.tensor_add(sb_dt[:, :], sb_tmpB[:, :], sb_tmpA[:, :])
                v.tensor_scalar_mul(sb_dA[:, :], sb_dt[:, :], sb_A[:, 0:1])
                for k in range(NCH):
                    v.tensor_tensor_scan(sb_cs[:, k * CH:(k + 1) * CH], psE[:, 0:128],
                                         sb_dA[:, k * CH:(k + 1) * CH], 0.0, ALU.mult, ALU.add)
                for k in range(NCH):
                    vedge(v, v.tensor_add(sb_sig[:, k + 1:k + 2], sb_sig[:, k:k + 1],
                                          sb_cs[:, k * CH + CH - 1: k * CH + CH]))
        C.add(sEV, 5)
        C.add(sDV, 1)

        MM0 = C.get(sMM)
        EV0 = C.get(sEV)
        DV0 = C.get(sDV)
        MM_PC, EV_PC, DV_PC = 77, 76, 85

        def mmD(db):
            return 14 + db if db <= 5 else 22 + 4 * ((db - 6) // 2) + ((db - 6) % 2)

        def mmY(bb):
            return 20 + 4 * bb if bb <= 12 else 46 + 2 * bb

        def mmS(bb):
            return mmY(bb) + 1

        def evD(db):
            return 13 + db if db <= 5 else 21 + 4 * ((db - 6) // 2) + ((db - 6) % 2)

        def evYo(bb):
            return 19 + 4 * bb if bb <= 12 else 45 + 2 * bb

        def evSc(bb):
            return evYo(bb) + 1

        def dvR(db):
            return 6 + db if db <= 5 else 15 + 5 * ((db - 6) // 2) + ((db - 6) % 2)

        def dvXds(bb):
            return 12 + 5 * bb if bb <= 12 else 38 + 3 * bb

        def dvYadd(bb):
            return dvXds(bb) + 1

        def dvSupd(bb):
            return dvXds(bb) + 2

        def _w(eng, sem, base, val):
            if val > 0:
                eng.wait_ge(sem, base + val)

        fills = []
        fills.append([("dtT", 0), ("dsT", 0), ("odT", 0), ("ecolT", 0)] + [("B", g) for g in range(4)])
        fills.append([("B", g) for g in range(4, 8)] + [("x", o) for o in range(4)])
        for f in range(7):
            fills.append([("x", o) for o in range(4 + f * 8, 12 + f * 8)])
        fills.append([("x", o) for o in range(60, 64)])

        def emit_D(t, k, db):
            if db == 0:
                for fi in range(4):
                    cFL.wait(t, 4 * k + fi)
                _w(t, sDV, DV0, k * DV_PC + 2)
            elif db == 1:
                _w(t, sDV, DV0, k * DV_PC + 3)
            else:
                _w(t, sEV, EV0, k * EV_PC + evD(db - 2))
            bank = psD[db % 2]
            for hh in range(4):
                h = db * 4 + hh
                sl = slice(hh * 128, (hh + 1) * 128)
                t.matmul(bank[:, sl], sb_ones2[:, :], FLc[0:2, h * 128:(h + 1) * 128],
                         start=True, stop=False)
                mi = t.matmul(bank[:, sl], FLn[0:2, h * 128:(h + 1) * 128], sb_ones2[:, :],
                              start=False, stop=True)
            mi.then_inc(sMM, 1)

        def emit_Y(t, k, bb):
            g_ = bb // 2
            bcb = sb_bcb[k % 2]
            _w(t, sDV, DV0, k * DV_PC + dvR(2 * bb + 1))
            _w(t, sDV, DV0, (k - 1) * DV_PC + dvSupd(bb))
            if bb >= 1:
                _w(t, sEV, EV0, k * EV_PC + evYo(bb - 1))
            else:
                _w(t, sEV, EV0, k * EV_PC + 12)
            t.matmul(psE[:, :], bcb[:, (8 + g_) * 128:(9 + g_) * 128],
                     S_sb[:, bb * 512:(bb + 1) * 512], start=True, stop=True)
            if bb >= 2:
                _w(t, sDV, DV0, k * DV_PC + dvYadd(bb - 2))
            else:
                _w(t, sDV, DV0, (k - 1) * DV_PC + dvYadd(bb + 14))
            yb = psY[bb % 2]
            for hh in range(8):
                h = bb * 8 + hh
                db = h // 4
                t.matmul(yb[:, hh * 64:(hh + 1) * 64],
                         Rb[db % 4][:, (h % 4) * 128:(h % 4 + 1) * 128],
                         xdt[:, h * 64:(h + 1) * 64],
                         start=(hh == 0), stop=False, skip_group_check=True)
            t.matmul(yb[:, :], sb_identb[:, :], xD[:, bb * 512:(bb + 1) * 512],
                     start=False, stop=True, skip_group_check=True).then_inc(sMM, 1)

        def emit_S(t, k, bb):
            g_ = bb // 2
            _w(t, sDV, DV0, k * DV_PC + dvXds(bb))
            if bb >= 1:
                _w(t, sEV, EV0, k * EV_PC + evSc(bb - 1))
            else:
                _w(t, sEV, EV0, (k - 1) * EV_PC + 76)
            t.matmul(psS[:, :], B_tok[:, g_ * 128:(g_ + 1) * 128],
                     sb_xds[bb % 2][:, :], start=True, stop=True).then_inc(sMM, 1)

        with nc.named_scope("P2"), nc.Block() as blk:
            @blk.sync
            def _(s):
                for k in range(NCH):
                    if k >= 1:
                        _w(s, sMM, MM0, (k - 1) * MM_PC + 11)
                    if k >= 2:
                        _w(s, sMM, MM0, (k - 1) * MM_PC)
                    cL.issue(s, sb_xch[k % 2][:, :].rearrange("p (o t) -> p o t", o=64),
                             xc_sp[0:D_INNER, k * CH:(k + 1) * CH].rearrange("(o p) t -> p o t", p=128))
                    cL.issue(s, sb_bcb[k % 2][:, :].rearrange("p (o t) -> p o t", o=16),
                             xc_sp[D_INNER:XBC_OC, k * CH:(k + 1) * CH].rearrange("(o p) t -> p o t", p=128))
                    _w(s, sDV, DV0, k * DV_PC + 1)
                    if k >= 1:
                        _w(s, sMM, MM0, (k - 1) * MM_PC + mmD(31))
                    cFL.issue(s, FLn[0:1, :], sb_ncsh[:, :])
                    cFL.issue(s, FLn[1:2, :], sb_ncsl[:, :])
                    cFL.issue(s, FLc[0:1, :], sb_csh[:, :])
                    cFL.issue(s, FLc[1:2, :], sb_csl[:, :])

            @blk.gpsimd
            def _(g):
                for k in range(NCH):
                    _w(g, sDV, DV0, k * DV_PC + dvYadd(15))
                    cYR.issue(g, yr_sp[k * CH:(k + 1) * CH, :], ystage[:, :])
                cYR.wait_last(g)

            @blk.tensor
            def _(t):
                for k in range(NCH):
                    xch = sb_xch[k % 2]
                    bcb = sb_bcb[k % 2]
                    for j, fill in enumerate(fills):
                        if j == 0:
                            cL.wait(t, 2 * k)
                            cL.wait(t, 2 * k + 1)
                            _w(t, sDV, DV0, k * DV_PC + 1)
                            _w(t, sEV, EV0, k * EV_PC + 1)
                            _w(t, sEV, EV0, (k - 1) * EV_PC + 10)
                        elif j == 1:
                            _w(t, sEV, EV0, (k - 1) * EV_PC + 11)
                        else:
                            _w(t, sEV, EV0, k * EV_PC + j)
                        bank = psT[j % 2]
                        for sl, (kind, idx) in enumerate(fill):
                            dst = bank[:, sl * 128:(sl + 1) * 128]
                            if kind == "dtT":
                                mi = t.transpose(dst, sb_dtb16[:, :], sb_identb[:, :])
                            elif kind == "dsT":
                                mi = t.transpose(dst, sb_dsb[:, :], sb_identb[:, :])
                            elif kind == "odT":
                                mi = t.transpose(dst, sb_odb[:, :], sb_identb[:, :])
                            elif kind == "ecolT":
                                mi = t.transpose(bank[0:1, sl * 128:(sl + 1) * 128], sb_ecolb[:, :], sb_identb[:, :])
                            elif kind == "B":
                                mi = t.transpose(dst, bcb[:, idx * 128:(idx + 1) * 128], sb_identb[:, :])
                            else:
                                mi = t.transpose(dst, xch[:, idx * 128:(idx + 1) * 128], sb_identb[:, :])
                        mi.then_inc(sMM, 1)
                        if j == 1:
                            _w(t, sEV, EV0, k * EV_PC + 2)
                            _w(t, sEV, EV0, (k - 1) * EV_PC + 12)
                            t.matmul(psE[:, 0:128], sb_ones1[:, :], e_row[:, :],
                                     start=True, stop=True).then_inc(sMM, 1)
                    for half in range(2):
                        _w(t, sEV, EV0, (k - 1) * EV_PC + evD(30 + half))
                        for gg in range(4):
                            g_ = half * 4 + gg
                            mi = t.matmul(psD[half][:, gg * 128:(gg + 1) * 128],
                                          bcb[:, g_ * 128:(g_ + 1) * 128],
                                          bcb[:, (8 + g_) * 128:(9 + g_) * 128],
                                          start=True, stop=True, skip_group_check=True)
                        mi.then_inc(sMM, 1)
                    for db in range(6):
                        emit_D(t, k, db)
                    for bb in range(16):
                        emit_Y(t, k, bb)
                        emit_S(t, k, bb)
                        if bb <= 12:
                            for dd in range(2):
                                db = 6 + 2 * bb + dd
                                if db < 32:
                                    emit_D(t, k, db)

            @blk.scalar
            def _(sc):
                for k in range(NCH):
                    ch = slice(k * CH, (k + 1) * CH)
                    last = slice(k * CH + CH - 1, k * CH + CH)
                    _w(sc, sMM, MM0, (k - 1) * MM_PC + 1)
                    sc.activation(sb_dsb[:, :], sb_cs[:, ch], AF.Exp, scale=-1.0, bias=sb_cs[:, last])
                    sc.activation(sb_odb[:, :], sb_cs[:, ch], AF.Exp)
                    sc.activation(sb_ecolb[:, :], sb_cs[:, last], AF.Exp).then_inc(sEV, 1)
                    for j, fill in enumerate(fills):
                        _w(sc, sMM, MM0, k * MM_PC + j + 1 + (1 if j >= 2 else 0))
                        if j == 0:
                            _w(sc, sDV, DV0, (k - 1) * DV_PC + 85)
                        bank = psT[j % 2]
                        if j == 0:
                            dsts = [dtT_sb[:, :], dsT_sb[:, :], odT_sb[:, :], None] + \
                                   [B_tok[:, gg * 128:(gg + 1) * 128] for gg in range(4)]
                            for sl, d in enumerate(dsts):
                                if d is None:
                                    mi = sc.activation(e_row[:, :], bank[0:1, sl * 128:(sl + 1) * 128], AF.Copy)
                                else:
                                    mi = sc.activation(d, bank[:, sl * 128:(sl + 1) * 128], AF.Copy)
                        elif j == 1:
                            for sl in range(4):
                                mi = sc.activation(B_tok[:, (4 + sl) * 128:(5 + sl) * 128],
                                                   bank[:, sl * 128:(sl + 1) * 128], AF.Copy)
                            mi = sc.activation(x_tok[:, 0:512], bank[:, 512:1024], AF.Copy)
                        elif j < 9:
                            o0 = 4 + (j - 2) * 8
                            mi = sc.activation(x_tok[:, o0 * 128:(o0 + 8) * 128], bank[:, :], AF.Copy)
                        else:
                            mi = sc.activation(x_tok[:, 60 * 128:64 * 128], bank[:, 0:512], AF.Copy)
                        mi.then_inc(sEV, 1)
                    _w(sc, sMM, MM0, k * MM_PC + 3)
                    _w(sc, sDV, DV0, (k - 1) * DV_PC + 85)
                    sc.activation(E_bc[:, :], psE[:, 0:128], AF.Copy).then_inc(sEV, 1)

                    def emit_Dexp(db):
                        _w(sc, sMM, MM0, k * MM_PC + mmD(db))
                        if db >= 4:
                            _w(sc, sDV, DV0, k * DV_PC + dvR(db - 4))
                        else:
                            _w(sc, sDV, DV0, (k - 1) * DV_PC + dvR(28 + db))
                        sc.activation(Fb[db % 4][:, :], psD[db % 2][:, :], AF.Exp).then_inc(sEV, 1)

                    for db in range(6):
                        emit_Dexp(db)
                    for bb in range(16):
                        _w(sc, sMM, MM0, k * MM_PC + mmY(bb))
                        if bb >= 2:
                            _w(sc, sDV, DV0, k * DV_PC + dvYadd(bb - 2))
                        else:
                            _w(sc, sDV, DV0, (k - 1) * DV_PC + dvYadd(bb + 14))
                        sc.activation(sb_yo[bb % 2][:, :], psE[:, :], AF.Copy).then_inc(sEV, 1)
                        _w(sc, sMM, MM0, k * MM_PC + mmS(bb))
                        if bb >= 2:
                            _w(sc, sDV, DV0, k * DV_PC + dvSupd(bb - 2))
                        else:
                            _w(sc, sDV, DV0, (k - 1) * DV_PC + dvSupd(bb + 14))
                        sc.activation(sb_sc[bb % 2][:, :], psS[:, :], AF.Copy).then_inc(sEV, 1)
                        if bb <= 12:
                            for dd in range(2):
                                db = 6 + 2 * bb + dd
                                if db < 32:
                                    emit_Dexp(db)

            @blk.vector
            def _(v):
                for k in range(NCH):
                    ch = slice(k * CH, (k + 1) * CH)
                    _w(v, sMM, MM0, (k - 1) * MM_PC + 1)
                    if k >= 1:
                        for fi in range(4):
                            cFL.wait(v, 4 * (k - 1) + fi)
                    v.tensor_copy(sb_dtb16[:, :], sb_dt[:, ch])
                    v.tensor_copy(sb_csh[:, :], sb_cs[:, ch])
                    v.tensor_tensor(sb_csl[:, :], sb_cs[:, ch], sb_csh[:, :], ALU.subtract)
                    v.tensor_scalar_mul(sb_ncsh[:, :], sb_csh[:, :], -1.0)
                    v.tensor_scalar_mul(sb_ncsl[:, :], sb_csl[:, :], -1.0).then_inc(sDV, 1)
                    for half in range(2):
                        _w(v, sMM, MM0, k * MM_PC + 12 + half)
                        for gg in range(4):
                            g_ = half * 4 + gg
                            mi = v.tensor_tensor(sb_Gm[:, g_ * 128:(g_ + 1) * 128],
                                                 psD[half][:, gg * 128:(gg + 1) * 128],
                                                 sb_maskb[:, :], ALU.mult)
                        mi.then_inc(sDV, 1)
                    _w(v, sEV, EV0, k * EV_PC + 11)
                    _w(v, sMM, MM0, (k - 1) * MM_PC + 77)
                    x3 = x_tok[:, :].rearrange("p (h q) -> p h q", h=128)
                    v.tensor_tensor(xdt[:, :].rearrange("p (h q) -> p h q", h=128), x3,
                                    dtT_sb[:, :].unsqueeze(2).broadcast_to([128, 128, 64]),
                                    ALU.mult).then_inc(sDV, 1)
                    if k >= 1:
                        cYR.wait(v, k - 1)
                    v.tensor_tensor(xD[:, :].rearrange("p (h q) -> p h q", h=128), x3,
                                    sb_Db[:, :].unsqueeze(2).broadcast_to([128, 128, 64]),
                                    ALU.mult).then_inc(sDV, 1)

                    def emit_R(db):
                        _w(v, sEV, EV0, k * EV_PC + evD(db))
                        if db >= 4:
                            _w(v, sMM, MM0, k * MM_PC + mmY((db - 4) // 2))
                        else:
                            _w(v, sMM, MM0, (k - 1) * MM_PC + mmY(14 + db // 2))
                        g_ = db // 4
                        v.scalar_tensor_tensor(
                            Rb[db % 4][:, :].rearrange("p (a b) -> p a b", a=4),
                            Fb[db % 4][:, :].rearrange("p (a b) -> p a b", a=4), 1.0,
                            sb_Gm[:, g_ * 128:(g_ + 1) * 128].unsqueeze(1).broadcast_to([128, 4, 128]),
                            ALU.min, ALU.mult).then_inc(sDV, 1)

                    for db in range(6):
                        emit_R(db)
                    for bb in range(16):
                        g_ = bb // 2
                        sl = slice(bb * 512, (bb + 1) * 512)
                        if bb >= 2:
                            _w(v, sMM, MM0, k * MM_PC + mmS(bb - 2))
                        else:
                            _w(v, sMM, MM0, (k - 1) * MM_PC + mmS(bb + 14))
                        v.tensor_tensor(sb_xds[bb % 2][:, :].rearrange("p (a b) -> p a b", a=8),
                                        xdt[:, sl].rearrange("p (a b) -> p a b", a=8),
                                        dsT_sb[:, bb * 8:(bb + 1) * 8].unsqueeze(2).broadcast_to([128, 8, 64]),
                                        ALU.mult).then_inc(sDV, 1)
                        _w(v, sEV, EV0, k * EV_PC + evYo(bb))
                        v.tensor_tensor(sb_t2[:, :].rearrange("p (a b) -> p a b", a=8),
                                        sb_yo[bb % 2][:, :].rearrange("p (a b) -> p a b", a=8),
                                        odT_sb[:, bb * 8:(bb + 1) * 8].unsqueeze(2).broadcast_to([128, 8, 64]),
                                        ALU.mult)
                        v.tensor_tensor(ystage[:, sl], psY[bb % 2][:, :], sb_t2[:, :],
                                        ALU.add).then_inc(sDV, 1)
                        _w(v, sEV, EV0, k * EV_PC + evSc(bb))
                        _w(v, sMM, MM0, k * MM_PC + mmY(bb))
                        v.tensor_tensor(sb_t3[:, :].rearrange("p (a b) -> p a b", a=8),
                                        S_sb[:, sl].rearrange("p (a b) -> p a b", a=8),
                                        E_bc[:, bb * 8:(bb + 1) * 8].unsqueeze(2).broadcast_to([128, 8, 64]),
                                        ALU.mult)
                        v.tensor_tensor(S_sb[:, sl], sb_t3[:, :], sb_sc[bb % 2][:, :],
                                        ALU.add).then_inc(sDV, 1)
                        if bb <= 12:
                            for dd in range(2):
                                db = 6 + 2 * bb + dd
                                if db < 32:
                                    emit_R(db)

        C.add(sMM, NCH * MM_PC)
        C.add(sEV, NCH * EV_PC)
        C.add(sDV, NCH * DV_PC)

        if "p2" in debug:
            with nc.Block() as blk:
                @blk.sync
                def _(s):
                    cH0.issue(s, dbg["yr"][:, :], yr_sp[:, :])
                    cH0.issue(s, dbg["st"][:, :], S_sb[:, :])
                    cH0.wait_last(s)

        if "p29" not in phases:
            raise _StopBuild()

        with nc.named_scope("P29"), nc.Block() as blk:
            st_issue = len(cXC.marks)

            @blk.vector
            def _(v):
                v.tensor_scalar_mul(S_sb[:, :], S_sb[:, :], sb_role[:, 0:1]).then_inc(sDV, 1)
                cXC.wait(v, st_issue + 1)
                v.tensor_scalar_mul(S_sb[:, :], S_sb[:, :], sb_role[:, 1:2]).then_inc(sDV, 1)

            @blk.sync
            def _(s):
                s.wait_ge(sDV, C.get(sDV) + 1)
                cXC.issue(s, st_in[:, :], S_sb[:, :])

            @blk.gpsimd
            def _(g):
                cXC.wait(g, st_issue)
                g.collective_compute("AllReduce", ALU.add,
                                     replica_groups=[[0, 1], [2, 3], [4, 5], [6, 7]],
                                     ins=[st_in.ap().opt()], outs=[st_out.ap().opt()]).then_inc(sCC, 1)

            @blk.scalar
            def _(sc):
                sc.wait_ge(sCC, C.get(sCC) + 1)
                cXC.issue(sc, S_sb[:, :], st_out[:, :])
        C.add(sDV, 2)
        C.add(sCC, 1)

        if "p2c" not in phases:
            raise _StopBuild()

        _close(p2_ctx)

        p2c_ctx = _mkstack(_cl)
        cT = [p2c_ctx.enter_context(nc.psum_tensor(f"cT{i}", [128, 1024], BF16)) for i in range(2)]
        cP = [p2c_ctx.enter_context(nc.psum_tensor(f"cP{i}", [128, 512], F32)) for i in range(2)]
        ybuf = [p2c_ctx.enter_context(nc.sbuf_tensor(f"c2_y{i}", [128, D_INNER], BF16)) for i in range(2)]
        zbuf = [p2c_ctx.enter_context(nc.sbuf_tensor(f"c2_z{i}", [128, D_INNER], BF16)) for i in range(2)]
        cbuf = [p2c_ctx.enter_context(nc.sbuf_tensor(f"c2_c{i}", [128, 8 * 128], BF16)) for i in range(2)]
        yT_acc = p2c_ctx.enter_context(nc.sbuf_tensor("c2_yT", [128, 64 * 512], BF16))
        odb2 = p2c_ctx.enter_context(nc.sbuf_tensor("c2_od", [128, 128], BF16))
        odT2 = p2c_ctx.enter_context(nc.sbuf_tensor("c2_odT", [128, 128], BF16))
        cstg = [p2c_ctx.enter_context(nc.sbuf_tensor(f"c2_cs{i}", [128, 512], BF16)) for i in range(2)]
        ct2 = p2c_ctx.enter_context(nc.sbuf_tensor("c2_t2", [128, 512], BF16))
        ssum = p2c_ctx.enter_context(nc.sbuf_tensor("c2_ss", [128, 1], F32))

        cL2 = Chain(nc, ctx, "cL2", 3)
        cYT = Chain(nc, ctx, "cYT", 2)

        MM2, EV2, DV2 = 25, 28, 17
        M2 = C.get(sMM)
        E2 = C.get(sEV)
        D2 = C.get(sDV)

        with nc.named_scope("P2c"), nc.Block() as blk:
            @blk.sync
            def _(s):
                for k in range(NCH):
                    if k >= 2:
                        _w(s, sMM, M2, (k - 2) * MM2 + 25)
                        _w(s, sEV, E2, (k - 2) * EV2 + 27)
                    cL2.issue(s, ybuf[k % 2][:, :], yr_sp[k * CH:(k + 1) * CH, :])
                    cL2.issue(s, zbuf[k % 2][:, :], z_sp[k * CH:(k + 1) * CH, :])
                    cL2.issue(s, cbuf[k % 2][:, :].rearrange("p (o t) -> p o t", o=8),
                              xc_sp[D_INNER + 1024:XBC_OC, k * CH:(k + 1) * CH].rearrange("(o p) t -> p o t", p=128))

            @blk.tensor
            def _(t):
                for k in range(NCH):
                    _w(t, sEV, E2, k * EV2 + 1)
                    _w(t, sEV, E2, (k - 1) * EV2 + 25)
                    t.transpose(cT[0][:, 0:128], odb2[:, :], sb_identb[:, :]).then_inc(sMM, 1)
                    for bb in range(16):
                        g_ = bb // 2
                        if bb == 0:
                            cL2.wait(t, 3 * k + 2)
                        if bb >= 2:
                            _w(t, sEV, E2, k * EV2 + 1 + bb)
                        else:
                            _w(t, sEV, E2, (k - 1) * EV2 + 17 + bb)
                        t.matmul(cP[bb % 2][:, :], cbuf[k % 2][:, g_ * 128:(g_ + 1) * 128],
                                 S_sb[:, bb * 512:(bb + 1) * 512], start=True, stop=True).then_inc(sMM, 1)
                    _w(t, sDV, D2, k * DV2 + 17)
                    for f in range(8):
                        if f == 0:
                            _w(t, sEV, E2, k * EV2 + 2)
                        elif f == 1:
                            _w(t, sEV, E2, (k - 1) * EV2 + 26)
                        else:
                            _w(t, sEV, E2, k * EV2 + 19 + (f - 2))
                        for q in range(8):
                            o = f * 8 + q
                            mi = t.transpose(cT[f % 2][:, q * 128:(q + 1) * 128],
                                             ybuf[k % 2][:, o * 128:(o + 1) * 128], sb_identb[:, :])
                        mi.then_inc(sMM, 1)

            @blk.scalar
            def _(sc):
                for k in range(NCH):
                    ch = slice(k * CH, (k + 1) * CH)
                    _w(sc, sMM, M2, (k - 1) * MM2 + 1)
                    sc.activation(odb2[:, :], sb_cs[:, ch], AF.Exp, bias=sb_sig[:, k:k + 1]).then_inc(sEV, 1)
                    _w(sc, sMM, M2, k * MM2 + 1)
                    _w(sc, sDV, D2, (k - 1) * DV2 + 16)
                    sc.activation(odT2[:, :], cT[0][:, 0:128], AF.Copy).then_inc(sEV, 1)
                    for bb in range(16):
                        _w(sc, sMM, M2, k * MM2 + 2 + bb)
                        if bb >= 2:
                            _w(sc, sDV, D2, k * DV2 + bb - 1)
                        else:
                            _w(sc, sDV, D2, (k - 1) * DV2 + 15 + bb)
                        sc.activation(cstg[bb % 2][:, :], cP[bb % 2][:, :], AF.Copy).then_inc(sEV, 1)
                    for f in range(8):
                        _w(sc, sMM, M2, k * MM2 + 18 + f)
                        if f == 0 and k >= 4:
                            cYT.wait(sc, k // 4 - 1)
                        dst = yT_acc[:, :].rearrange("p (o kc t) -> p o kc t", o=64, kc=4)[:, f * 8:(f + 1) * 8, k % 4, :]
                        sc.activation(dst, cT[f % 2][:, :].rearrange("p (o t) -> p o t", o=8),
                                      AF.Copy).then_inc(sEV, 1)
                    _w(sc, sDV, D2, k * DV2 + 17)
                    sc.activation(zbuf[k % 2][:, :], ybuf[k % 2][:, :], AF.Square,
                                  accum_out=ssum[:, 0:1]).then_inc(sEV, 1)
                    sc.wait_ge(sEV, E2 + k * EV2 + 27)
                    sc.activation(sb_rs[:, k:k + 1], ssum[:, 0:1], AF.Ln,
                                  scale=1.0 / D_INNER, bias=sb_eps[:, 0:1]).then_inc(sEV, 1)
                    sc.wait_ge(sEV, E2 + k * EV2 + 28)
                    sc.activation(sb_rs[:, k:k + 1], sb_rs[:, k:k + 1], AF.Exp, scale=-0.5)

            @blk.vector
            def _(v):
                for k in range(NCH):
                    cL2.wait(v, 3 * k)
                    for bb in range(16):
                        sl = slice(bb * 512, (bb + 1) * 512)
                        _w(v, sEV, E2, k * EV2 + 3 + bb)
                        v.tensor_tensor(ct2[:, :].rearrange("p (a b) -> p a b", a=8),
                                        cstg[bb % 2][:, :].rearrange("p (a b) -> p a b", a=8),
                                        odT2[:, bb * 8:(bb + 1) * 8].unsqueeze(2).broadcast_to([128, 8, 64]),
                                        ALU.mult)
                        v.tensor_tensor(ybuf[k % 2][:, sl], ybuf[k % 2][:, sl], ct2[:, :],
                                        ALU.add).then_inc(sDV, 1)
                    cL2.wait(v, 3 * k + 1)
                    v.tensor_tensor(ybuf[k % 2][:, :], ybuf[k % 2][:, :], zbuf[k % 2][:, :],
                                    ALU.mult).then_inc(sDV, 1)

            @blk.gpsimd
            def _(g):
                for hf in range(2):
                    _w(g, sEV, E2, (4 * hf + 3) * EV2 + 26)
                    cYT.issue(g, yT_sp[:, hf * 512:(hf + 1) * 512].rearrange("(o p) t -> p o t", p=128),
                              yT_acc[:, :].rearrange("p (o t) -> p o t", o=64))
                cYT.wait_last(g)

        C.add(sMM, NCH * MM2)
        C.add(sEV, NCH * EV2)
        C.add(sDV, NCH * DV2)

        if "p2c" in debug:
            with nc.Block() as blk:
                @blk.sync
                def _(s):
                    cH0.issue(s, dbg["yT"][:, :], yT_sp[:, :])
                    cH0.wait_last(s)

        if "p3" not in phases:
            raise _StopBuild()

        _close(p2c_ctx)
        _close(s_ctx)
        p3_ctx = _mkstack(_cl)
        psO = [p3_ctx.enter_context(nc.psum_tensor(f"psO{i}", [128, 512], F32)) for i in range(8)]
        yTf = p3_ctx.enter_context(nc.sbuf_tensor("p3_yT", [128, 64 * 1024], BF16))
        wobuf = [p3_ctx.enter_context(nc.sbuf_tensor(f"p3_w{i}", [128, 16 * 512], BF16)) for i in range(2)]
        xrsl = [p3_ctx.enter_context(nc.sbuf_tensor(f"p3_xr{i}", [128, 8 * 512], BF16)) for i in range(2)]
        x1stg = [p3_ctx.enter_context(nc.sbuf_tensor(f"p3_s{i}", [128, 8 * 512], BF16)) for i in range(2)]
        p3tmp = p3_ctx.enter_context(nc.sbuf_tensor("p3_tmp", [128, 512], BF16))

        cYL = Chain(nc, ctx, "cYL", 2)
        cWO = Chain(nc, ctx, "cWO", 2)
        cXR = Chain(nc, ctx, "cXR", 2)
        cX1 = Chain(nc, ctx, "cX1", 2)

        M3 = C.get(sMM)
        D3 = C.get(sDV)
        MM3, DV3 = 32, 65

        with nc.named_scope("P3"), nc.Block() as blk:
            @blk.sync
            def _(s):
                for hf in range(2):
                    cYL.issue(s, yTf[:, :].rearrange("p (o h t) -> p o h t", o=64, h=2)[:, :, hf, :],
                              yT_sp[:, hf * 512:(hf + 1) * 512].rearrange("(o p) t -> p o t", p=128))
                for ob in range(8):
                    for tt8 in range(8):
                        _w(s, sDV, D3, 1 + ob * 8 + tt8 + 1)
                        if tt8 == 7:
                            cX1.issue(s,
                                      x1_sp[:, ob * 512:(ob + 1) * 512].rearrange("(tt p) c -> p tt c", p=128),
                                      x1stg[ob % 2][:, :].rearrange("p (tt c) -> p tt c", tt=8))

            @blk.gpsimd
            def _(g):
                for ob in range(8):
                    if ob >= 2:
                        _w(g, sDV, D3, 1 + (ob - 1) * 8)
                    cXR.issue(g, xrsl[ob % 2][:, :].rearrange("p (tt c) -> p tt c", tt=8),
                              d_xres[:, ob * 512:(ob + 1) * 512].rearrange("(tt p) c -> p tt c", p=128))
                    for cq in range(4):
                        bi = ob * 4 + cq
                        if bi >= 2:
                            _w(g, sMM, M3, bi - 1)
                        cWO.issue(g, wobuf[bi % 2][:, :].rearrange("p (k c) -> p k c", k=16),
                                  d_wo[cq * 2048:(cq + 1) * 2048, ob * 512:(ob + 1) * 512].rearrange("(k p) c -> p k c", p=128))

            @blk.vector
            def _(v):
                cYL.wait(v, 0)
                cYL.wait(v, 1)
                for ct in range(64):
                    mi = v.tensor_scalar_mul(yTf[:, ct * 1024:(ct + 1) * 1024],
                                             yTf[:, ct * 1024:(ct + 1) * 1024], sb_nrmw[:, ct:ct + 1])
                mi.then_inc(sDV, 1)
                for ob in range(8):
                    cXR.wait(v, ob)
                    for tt8 in range(8):
                        _w(v, sMM, M3, ob * 4 + 4)
                        if ob >= 2 and tt8 == 0:
                            cX1.wait(v, ob - 2)
                        hf, tt = tt8 // 4, tt8 % 4
                        v.tensor_scalar_mul(p3tmp[:, :], psO[tt8][:, :],
                                            sb_rs[:, hf * 4 + tt: hf * 4 + tt + 1])
                        v.scalar_tensor_tensor(
                            x1stg[ob % 2][:, tt8 * 512:(tt8 + 1) * 512],
                            xrsl[ob % 2][:, tt8 * 512:(tt8 + 1) * 512], 2.0,
                            p3tmp[:, :], ALU.mult, ALU.add).then_inc(sDV, 1)

            @blk.tensor
            def _(t):
                _w(t, sDV, D3, 1)
                for ob in range(8):
                    for cq in range(4):
                        bi = ob * 4 + cq
                        cWO.wait(t, bi)
                        wb = wobuf[bi % 2][:, :].rearrange("p (k c) -> p k c", k=16)
                        for kk in range(16):
                            ct = cq * 16 + kk
                            for tt8 in range(8):
                                if cq == 0 and kk == 0 and ob >= 1:
                                    _w(t, sDV, D3, 1 + (ob - 1) * 8 + tt8 + 1)
                                mi = t.matmul(psO[tt8][:, :],
                                              yTf[:, ct * 1024 + tt8 * 128: ct * 1024 + (tt8 + 1) * 128],
                                              wb[:, kk, :],
                                              start=(cq == 0 and kk == 0),
                                              stop=(cq == 3 and kk == 15))
                        mi.then_inc(sMM, 1)

        C.add(sMM, MM3)
        C.add(sDV, DV3)

        if "p3" in debug:
            with nc.Block() as blk:
                @blk.sync
                def _(s):
                    cH0.issue(s, dbg["x1"][:, :], x1_sp[:, :])
                    cH0.wait_last(s)

        if "p4" not in phases:
            raise _StopBuild()

        _close(p3_ctx)
        p4_ctx = _mkstack(_cl)
        x1T = p4_ctx.enter_context(nc.sbuf_tensor("p4_x1T", [128, KT * TOK], BF16))
        wgb = p4_ctx.enter_context(nc.sbuf_tensor("p4_wg", [128, KT * 8], BF16))
        aTb = p4_ctx.enter_context(nc.sbuf_tensor("p4_aT", [128, KT * 128], BF16))
        bTfb = p4_ctx.enter_context(nc.sbuf_tensor("p4_bT", [128, D_MODEL], BF16))
        sb_bbase = p4_ctx.enter_context(nc.sbuf_tensor("p4_bb", [1, D_MODEL], BF16))
        logtok = p4_ctx.enter_context(nc.sbuf_tensor("p4_lg", [128, 64], F32))
        route = p4_ctx.enter_context(nc.sbuf_tensor("p4_rt", [128, 64], BF16))
        r_m1 = p4_ctx.enter_context(nc.sbuf_tensor("p4_m1", [128, 8], F32))
        r_sc = p4_ctx.enter_context(nc.sbuf_tensor("p4_sc", [128, 8 * 4], F32))
        hwT = p4_ctx.enter_context(nc.sbuf_tensor("p4_hwT", [128, TOK], BF16))
        hwstg = p4_ctx.enter_context(nc.sbuf_tensor("p4_hws", [128, 128], BF16))
        p4a_ctx = _mkstack(_cl)
        pT4 = [p4a_ctx.enter_context(nc.psum_tensor(f"pT4{i}", [128, 1024], BF16)) for i in range(2)]
        pL4 = [p4a_ctx.enter_context(nc.psum_tensor(f"pL4{i}", [128, 512], F32)) for i in range(2)]
        x1tl = [p4a_ctx.enter_context(nc.sbuf_tensor(f"p4_x1{i}", [128, D_MODEL], BF16)) for i in range(2)]
        x1ntl = [p4a_ctx.enter_context(nc.sbuf_tensor(f"p4_xn{i}", [128, D_MODEL], BF16)) for i in range(2)]
        dump4 = p4a_ctx.enter_context(nc.sbuf_tensor("p4_dump", [128, D_MODEL], BF16))
        s1c = p4a_ctx.enter_context(nc.sbuf_tensor("p4_s1", [128, 1], F32))
        s2c = p4a_ctx.enter_context(nc.sbuf_tensor("p4_s2", [128, 1], F32))

        cX2 = Chain(nc, ctx, "cX2", 2)
        cW4 = Chain(nc, ctx, "cW4", 2)

        with nc.named_scope("P4a"), nc.Block() as blk:
            @blk.gpsimd
            def _(g):
                cW4.issue(g, wgb[:, :], d_wg[:, :])
                cW4.issue(g, aTb[:, :], d_aT[:, :])
                cW4.issue(g, bTfb[:, :], d_bTf[:, :])
                cW4.issue(g, sb_bbase[:, :], d_bbase[:, :])

            @blk.sync
            def _(s):
                e0 = C.get(sEV)
                d0 = C.get(sDV)
                for tt in range(8):
                    if tt >= 2:
                        _w(s, sEV, e0, 8 * (tt - 2) + 2)
                        _w(s, sDV, d0, 2 * (tt - 2) + 2)
                    cX2.issue(s, x1tl[tt % 2][:, :], x1_sp[tt * 128:(tt + 1) * 128, :])

            @blk.scalar
            def _(sc):
                e0 = C.get(sEV)
                m0 = C.get(sMM)
                d0 = C.get(sDV)
                for tt in range(8):
                    cX2.wait(sc, tt)
                    if tt >= 1:
                        _w(sc, sDV, d0, 2 * (tt - 1) + 1)
                    sc.activation(dump4[:, :], x1tl[tt % 2][:, :], AF.Copy,
                                  accum_out=s1c[:, 0:1]).then_inc(sEV, 1)
                    sc.wait_ge(sEV, e0 + 8 * tt + 1)
                    sc.activation(dump4[:, :], x1tl[tt % 2][:, :], AF.Square,
                                  accum_out=s2c[:, 0:1]).then_inc(sEV, 1)
                    sc.wait_ge(sDV, d0 + 2 * tt + 1)
                    sc.activation(sb_rstd[:, tt:tt + 1], r_sc[:, 0:1], AF.Ln, bias=sb_eps[:, 0:1]).then_inc(sEV, 1)
                    sc.wait_ge(sEV, e0 + 8 * tt + 3)
                    sc.activation(sb_rstd[:, tt:tt + 1], sb_rstd[:, tt:tt + 1], AF.Exp,
                                  scale=-0.5).then_inc(sEV, 1)
                    for f in range(4):
                        _w(sc, sMM, m0, tt * 4 + f + 1)
                        dst = x1T[:, :].rearrange("p (k t) -> p k t", k=KT)[:, f * 8:(f + 1) * 8, tt * 128:(tt + 1) * 128]
                        sc.activation(dst, pT4[f % 2][:, :].rearrange("p (k t) -> p k t", k=8),
                                      AF.Copy).then_inc(sEV, 1)

            @blk.vector
            def _(v):
                d0 = C.get(sDV)
                e0 = C.get(sEV)
                m0 = C.get(sMM)
                for tt in range(8):
                    _w(v, sEV, e0, 8 * tt + 2)
                    vedge(v, v.tensor_scalar_mul(sb_mean[:, tt:tt + 1], s1c[:, 0:1], 1.0 / D_MODEL))
                    vedge(v, v.tensor_scalar_mul(r_sc[:, 1:2], s2c[:, 0:1], 1.0 / D_MODEL))
                    vedge(v, v.tensor_tensor(r_sc[:, 2:3], sb_mean[:, tt:tt + 1], sb_mean[:, tt:tt + 1], ALU.mult))
                    v.tensor_tensor(r_sc[:, 0:1], r_sc[:, 1:2], r_sc[:, 2:3], ALU.subtract).then_inc(sDV, 1)
                    _w(v, sEV, e0, 8 * tt + 4)
                    if tt >= 2:
                        _w(v, sMM, m0, 4 * (tt - 2) + 4)
                    v.tensor_scalar(x1ntl[tt % 2][:, :], x1tl[tt % 2][:, :],
                                    sb_mean[:, tt:tt + 1], sb_rstd[:, tt:tt + 1],
                                    ALU.subtract, ALU.mult).then_inc(sDV, 1)

            @blk.tensor
            def _(t):
                d0 = C.get(sDV)
                e0 = C.get(sEV)
                for tt in range(8):
                    _w(t, sDV, d0, 2 * tt + 2)
                    for f in range(4):
                        fi = tt * 4 + f
                        if fi >= 2:
                            pt, pf = (fi - 2) // 4, (fi - 2) % 4
                            _w(t, sEV, e0, 8 * pt + 5 + pf)
                        for q in range(8):
                            kk = f * 8 + q
                            mi = t.transpose(pT4[f % 2][:, q * 128:(q + 1) * 128],
                                             x1ntl[tt % 2][:, kk * 128:(kk + 1) * 128], sb_identb[:, :])
                        mi.then_inc(sMM, 1)

        C.add(sMM, 32)
        C.add(sEV, 8 * 8)
        C.add(sDV, 16)

        with nc.named_scope("P4b"), nc.Block() as blk:
            @blk.tensor
            def _(t):
                m0 = C.get(sMM)
                cW4.wait(t, 0)
                x1Tv = x1T[:, :].rearrange("p (k t) -> p k t", k=KT)
                wgv = wgb[:, :].rearrange("p (k e) -> p k e", k=KT)
                for tt in range(8):
                    if tt >= 2:
                        _w(t, sEV, C.get(sEV), tt - 1)
                    for k in range(KT):
                        mi = t.matmul(pL4[tt % 2][:, 0:8], x1Tv[:, k, tt * 128:(tt + 1) * 128],
                                      wgv[:, k, :], start=(k == 0), stop=(k == KT - 1))
                    mi.then_inc(sMM, 1)

            @blk.scalar
            def _(sc):
                for tt in range(8):
                    _w(sc, sMM, C.get(sMM), tt + 1)
                    sc.activation(logtok[:, tt * 8:(tt + 1) * 8], pL4[tt % 2][:, 0:8],
                                  AF.Copy).then_inc(sEV, 1)
        C.add(sMM, 8)
        C.add(sEV, 8)

        with nc.named_scope("P4c"), nc.Block() as blk:
            @blk.vector
            def _(v):
                d0 = C.get(sDV)
                e0 = C.get(sEV)
                for tt in range(8):
                    lg = logtok[:, tt * 8:(tt + 1) * 8]
                    m1 = r_m1[:, tt:tt + 1]
                    lm = r_sc[:, 0:8]
                    mi = v.reduce_max(m1, lg, AX.X)
                    vedge(v, mi)
                    v.nop().then_inc(sDV, 1)
                    vedge(v, v.tensor_scalar(lm, lg, m1, None, ALU.subtract))
                    vedge(v, v.tensor_scalar(r_sc[:, 8:16], lm, 0.0, None, ALU.is_ge))
                    vedge(v, v.scalar_tensor_tensor(r_sc[:, 16:24], r_sc[:, 8:16], -1e9, lm, ALU.mult, ALU.add))
                    mi = v.reduce_max(r_sc[:, 24:25], r_sc[:, 16:24], AX.X)
                    vedge(v, mi)
                    v.nop().then_inc(sDV, 1)
                    _w(v, sEV, e0, 2 * tt + 2)
                    vedge(v, v.tensor_scalar(r_sc[:, 8:16], lm, r_sc[:, 24:25], None, ALU.is_ge))
                    vedge(v, v.tensor_tensor(r_sc[:, 16:24], r_sc[:, 16:24], r_sc[:, 8:16], ALU.mult))
                    vedge(v, v.tensor_scalar_add(r_sc[:, 26:27], r_sc[:, 25:26], 1.0))
                    vedge(v, v.reciprocal(r_sc[:, 27:28], r_sc[:, 26:27]))
                    v.tensor_scalar_mul(route[:, tt * 8:(tt + 1) * 8], r_sc[:, 16:24],
                                        r_sc[:, 27:28]).then_inc(sDV, 1)

            @blk.scalar
            def _(sc):
                d0 = C.get(sDV)
                for tt in range(8):
                    _w(sc, sDV, d0, 3 * tt + 2)
                    sc.activation(r_sc[:, 16:24], r_sc[:, 0:8], AF.Exp).then_inc(sEV, 1)
                    sc.activation(r_sc[:, 25:26], r_sc[:, 24:25], AF.Exp).then_inc(sEV, 1)
        C.add(sDV, 24)
        C.add(sEV, 16)

        with nc.named_scope("P4d"), nc.Block() as blk:
            @blk.tensor
            def _(t):
                m0 = C.get(sMM)
                cW4.wait(t, 1)
                x1Tv = x1T[:, :].rearrange("p (k t) -> p k t", k=KT)
                aTv = aTb[:, :].rearrange("p (k r) -> p k r", k=KT)
                for tt in range(8):
                    if tt >= 2:
                        _w(t, sDV, C.get(sDV), tt - 1)
                    for k in range(KT):
                        mi = t.matmul(pL4[tt % 2][:, 0:128], x1Tv[:, k, tt * 128:(tt + 1) * 128],
                                      aTv[:, k, :], start=(k == 0), stop=(k == KT - 1))
                    mi.then_inc(sMM, 1)
                    _w(t, sDV, C.get(sDV), tt + 1)
                    if tt >= 1:
                        _w(t, sEV, C.get(sEV), tt)
                    t.transpose(pT4[0][:, 0:128], hwstg[:, :], sb_identb[:, :]).then_inc(sMM, 1)

            @blk.vector
            def _(v):
                m0 = C.get(sMM)
                for tt in range(8):
                    _w(v, sMM, m0, 2 * tt + 1)
                    v.tensor_tensor(hwstg[:, :].rearrange("p (e r) -> p e r", e=8),
                                    pL4[tt % 2][:, 0:128].rearrange("p (e r) -> p e r", e=8),
                                    route[:, tt * 8:(tt + 1) * 8].unsqueeze(2).broadcast_to([128, 8, 16]),
                                    ALU.mult).then_inc(sDV, 1)

            @blk.scalar
            def _(sc):
                m0 = C.get(sMM)
                for tt in range(8):
                    _w(sc, sMM, m0, 2 * tt + 2)
                    sc.activation(hwT[:, tt * 128:(tt + 1) * 128], pT4[0][:, 0:128],
                                  AF.Copy).then_inc(sEV, 1)
        C.add(sMM, 16)
        C.add(sDV, 8)
        C.add(sEV, 8)

        _close(p4a_ctx)
        psF = [p4_ctx.enter_context(nc.psum_tensor(f"psF{i}", [128, 512], F32)) for i in range(8)]
        wb4 = [p4_ctx.enter_context(nc.sbuf_tensor(f"p4_wb{i}", [128, 16 * 512], BF16)) for i in range(2)]
        x2stg = p4_ctx.enter_context(nc.sbuf_tensor("p4_x2s", [128, 8 * 512], BF16))
        x1rld = [p4_ctx.enter_context(nc.sbuf_tensor(f"p4_rl{i}", [128, 8 * 512], BF16)) for i in range(2)]
        cW5 = Chain(nc, ctx, "cW5", 2)
        cX3 = Chain(nc, ctx, "cX3", 2)
        cX4 = Chain(nc, ctx, "cX4", 2)

        M4 = C.get(sMM)
        D4 = C.get(sDV)
        MM4, DV4 = 9, 8

        with nc.named_scope("P4e"), nc.Block() as blk:
            @blk.gpsimd
            def _(g):
                for ob in range(8):
                    for ch in range(2):
                        bi = ob * 2 + ch
                        if bi >= 2:
                            obp, chp = (bi - 2) // 2, (bi - 2) % 2
                            _w(g, sMM, M4, obp * 9 + 1 if chp == 0 else (obp + 1) * 9)
                        cW5.issue(g, wb4[bi % 2][:, :].rearrange("p (k c) -> p k c", k=16),
                                  d_wb[ch * 2048:(ch + 1) * 2048, ob * 512:(ob + 1) * 512].rearrange("(k p) c -> p k c", p=128))

            @blk.sync
            def _(s):
                for ob in range(8):
                    if ob >= 2:
                        _w(s, sDV, D4, (ob - 1) * DV4)
                    cX3.issue(s, x1rld[ob % 2][:, :].rearrange("p (tt c) -> p tt c", tt=8),
                              x1_sp[:, ob * 512:(ob + 1) * 512].rearrange("(tt p) c -> p tt c", p=128))
                    _w(s, sDV, D4, ob * DV4 + 8)
                    cX4.issue(s, x2_sp[:, ob * 512:(ob + 1) * 512].rearrange("(tt p) c -> p tt c", p=128),
                              x2stg[:, :].rearrange("p (tt c) -> p tt c", tt=8))

            @blk.tensor
            def _(t):
                cW4.wait(t, 2)
                cW4.wait(t, 3)
                x1Tv = x1T[:, :].rearrange("p (k t) -> p k t", k=KT)
                for ob in range(8):
                    for ch in range(2):
                        bi = ob * 2 + ch
                        cW5.wait(t, bi)
                        wv = wb4[bi % 2][:, :].rearrange("p (k c) -> p k c", k=16)
                        for kk in range(16):
                            for tt in range(8):
                                if ch == 0 and kk == 0 and ob >= 1:
                                    _w(t, sDV, D4, (ob - 1) * DV4 + tt + 1)
                                mi = t.matmul(psF[tt][:, :],
                                              x1Tv[:, ch * 16 + kk, tt * 128:(tt + 1) * 128],
                                              wv[:, kk, :],
                                              start=(ch == 0 and kk == 0), stop=False,
                                              skip_group_check=True)
                        if ch == 0:
                            mi.then_inc(sMM, 1)
                    for tt in range(8):
                        t.matmul(psF[tt][:, :], hwT[:, tt * 128:(tt + 1) * 128],
                                 bTfb[:, ob * 512:(ob + 1) * 512], start=False, stop=False,
                                 skip_group_check=True)
                        t.matmul(psF[tt][:, :], sb_ones1[:, :], sb_bbase[0:1, ob * 512:(ob + 1) * 512],
                                 start=False, stop=True, skip_group_check=True).then_inc(sMM, 1)

            @blk.vector
            def _(v):
                for ob in range(8):
                    cX3.wait(v, ob)
                    if ob >= 1:
                        cX4.wait(v, ob - 1)
                    for tt in range(8):
                        _w(v, sMM, M4, ob * 9 + tt + 2)
                        v.tensor_scalar(x1rld[ob % 2][:, tt * 512:(tt + 1) * 512],
                                        x1rld[ob % 2][:, tt * 512:(tt + 1) * 512],
                                        sb_mean[:, tt:tt + 1], sb_rstd[:, tt:tt + 1],
                                        ALU.subtract, ALU.mult)
                        v.tensor_tensor(x2stg[:, tt * 512:(tt + 1) * 512],
                                        x1rld[ob % 2][:, tt * 512:(tt + 1) * 512],
                                        psF[tt][:, :], ALU.add).then_inc(sDV, 1)
        C.add(sMM, 8 * MM4)
        C.add(sDV, 8 * DV4)

        if "p4" in debug:
            with nc.Block() as blk:
                @blk.sync
                def _(s):
                    cH0.issue(s, dbg["x2"][:, :], x2_sp[:, :])
                    cH0.issue(s, dbg["x1T"][:, :], x1T[:, :])
                    cH0.issue(s, dbg["rt"][:, :], route[:, :])
                    cH0.issue(s, dbg["hwT"][:, :], hwT[:, :])
                    cH0.wait_last(s)

        if "p5" not in phases:
            raise _StopBuild()

        _close(p4_ctx)
        p5_ctx = _mkstack(_cl)
        x2tl = [p5_ctx.enter_context(nc.sbuf_tensor(f"p5_x2{i}", [128, D_MODEL], BF16)) for i in range(2)]
        ostg = [p5_ctx.enter_context(nc.sbuf_tensor(f"p5_o{i}", [128, D_MODEL], F32)) for i in range(2)]
        dump5 = p5_ctx.enter_context(nc.sbuf_tensor("p5_dump", [128, D_MODEL], BF16))
        s1c = p5_ctx.enter_context(nc.sbuf_tensor("p5_s1", [128, 1], F32))
        s2c = p5_ctx.enter_context(nc.sbuf_tensor("p5_s2", [128, 1], F32))
        cX5 = Chain(nc, ctx, "cX5", 2)
        cOUT = Chain(nc, ctx, "cOUT", 2)

        with nc.named_scope("P5"), nc.Block() as blk:
            @blk.sync
            def _(s):
                for tt in range(8):
                    if tt >= 2:
                        _w(s, sDV, C.get(sDV), 2 * (tt - 2) + 2)
                    cX5.issue(s, x2tl[tt % 2][:, :], x2_sp[tt * 128:(tt + 1) * 128, :])
                    _w(s, sDV, C.get(sDV), 2 * tt + 2)
                    cOUT.issue(s, d_out[tt * 128:(tt + 1) * 128, :], ostg[tt % 2][:, :])
                cOUT.wait_last(s)

            @blk.scalar
            def _(sc):
                e0 = C.get(sEV)
                for tt in range(8):
                    cX5.wait(sc, tt)
                    if tt >= 1:
                        sc.wait_ge(sDV, C.get(sDV) + 2 * (tt - 1) + 1)
                    sc.activation(dump5[:, :], x2tl[tt % 2][:, :], AF.Copy,
                                  accum_out=s1c[:, 0:1]).then_inc(sEV, 1)
                    sc.wait_ge(sEV, e0 + 4 * tt + 1)
                    sc.activation(dump5[:, :], x2tl[tt % 2][:, :], AF.Square,
                                  accum_out=s2c[:, 0:1]).then_inc(sEV, 1)
                    sc.wait_ge(sDV, C.get(sDV) + 2 * tt + 1)
                    sc.activation(sb_mean2[:, 1:2], sb_mean2[:, 2:3], AF.Ln, bias=sb_eps[:, 0:1]).then_inc(sEV, 1)
                    sc.wait_ge(sEV, e0 + 4 * tt + 3)
                    sc.activation(sb_mean2[:, 1:2], sb_mean2[:, 1:2], AF.Exp, scale=-0.5).then_inc(sEV, 1)

            @blk.vector
            def _(v):
                d0 = C.get(sDV)
                e0 = C.get(sEV)
                for tt in range(8):
                    _w(v, sEV, e0, 4 * tt + 2)
                    vedge(v, v.tensor_scalar_mul(sb_mean2[:, 0:1], s1c[:, 0:1], 1.0 / D_MODEL))
                    vedge(v, v.tensor_scalar_mul(sb_mean2[:, 3:4], s2c[:, 0:1], 1.0 / D_MODEL))
                    vedge(v, v.tensor_tensor(sb_mean2[:, 4:5], sb_mean2[:, 0:1], sb_mean2[:, 0:1], ALU.mult))
                    v.tensor_tensor(sb_mean2[:, 2:3], sb_mean2[:, 3:4], sb_mean2[:, 4:5],
                                    ALU.subtract).then_inc(sDV, 1)
                    _w(v, sEV, e0, 4 * tt + 4)
                    if tt >= 2:
                        cOUT.wait(v, tt - 2)
                    v.tensor_scalar(ostg[tt % 2][:, :], x2tl[tt % 2][:, :],
                                    sb_mean2[:, 0:1], sb_mean2[:, 1:2],
                                    ALU.subtract, ALU.mult).then_inc(sDV, 1)
        C.add(sDV, 16)
        C.add(sEV, 32)

      except _StopBuild:
        pass
      for st in reversed(open_stacks):
          st.close()
      open_stacks.clear()
    nc.finalize()
    return nc, dbg



def _ktile_major(a):
    k128, n = a.shape
    k = k128 // 128
    return np.ascontiguousarray(a.reshape(k, 128, n).transpose(1, 0, 2).reshape(128, k * n))


def make_inputs(full):
    hs = np.asarray(full["L0_attn_hidden_states"], dtype=np.float32)
    wip = np.asarray(full["L0_attn_in_proj_w"], np.float32)
    convw = np.asarray(full["L0_attn_conv_w"], np.float32)
    convb = np.asarray(full["L0_attn_conv_b"], np.float32)
    dtb = np.asarray(full["L0_attn_dt_bias"], np.float32)
    alog = np.asarray(full["L0_attn_A_log"], np.float32)
    dvec = np.asarray(full["L0_attn_D"], np.float32)
    nrmw = np.asarray(full["L0_attn_norm_w"], np.float32)
    wo = np.asarray(full["L0_attn_out_proj_w"], np.float32)
    wb = np.asarray(full["L0_ffn_W_base"], np.float32)
    bb = np.asarray(full["L0_ffn_b_base"], np.float32)
    wg = np.asarray(full["L0_ffn_W_gate"], np.float32)
    A = np.asarray(full["L0_ffn_A"], np.float32)
    B = np.asarray(full["L0_ffn_B"], np.float32)

    shared = {
        "wipT": np.ascontiguousarray(wip.T),
        "woT": np.ascontiguousarray(wo.T),
        "wbT": np.ascontiguousarray(wb.T),
        "wgT": _ktile_major(np.ascontiguousarray(wg.T)),
        "aT": _ktile_major(np.ascontiguousarray(A.reshape(128, 4096).T)),
        "bTf": np.ascontiguousarray(B.transpose(0, 2, 1).reshape(128, 4096)),
        "convw": np.ascontiguousarray(convw.reshape(80, 128, 4).transpose(1, 0, 2).reshape(128, 320)),
        "convb": np.ascontiguousarray(convb.reshape(80, 128).T),
        "dtb": np.ascontiguousarray(dtb.reshape(128, 1)),
        "alog": np.ascontiguousarray(alog.reshape(128, 1)),
        "drow": np.ascontiguousarray(dvec.reshape(1, 128)),
        "nrmw": np.ascontiguousarray(nrmw.reshape(64, 128).T),
        "bbase": np.ascontiguousarray(bb.reshape(1, 4096)),
        "maskut": np.triu(np.ones((128, 128), np.float32)),
        "ident": np.eye(128, dtype=np.float32),
    }
    in_maps = []
    for c in range(8):
        b, h = c // 2, c % 2
        shard = hs[b, h * TOK:(h + 1) * TOK, :]
        role = np.zeros((128, 2), np.float32)
        role[:, 0] = 1.0 if h == 0 else 0.0
        role[:, 1] = 1.0 if h == 1 else 0.0
        m = dict(shared)
        m["xT"] = _ktile_major(np.ascontiguousarray(shard.T))
        m["xres"] = np.ascontiguousarray(shard)
        m["role"] = role
        in_maps.append(m)
    return in_maps


_CACHE = {}


def kernel(**inputs):
    if "nc" not in _CACHE:
        nc, _ = build_program()
        _CACHE["nc"] = nc
    nc = _CACHE["nc"]
    in_maps = make_inputs(inputs)
    res = run_bass_kernel_spmd(nc, in_maps, core_ids=list(range(8)))
    out = np.empty((4, 2048, 4096), np.float32)
    for c in range(8):
        b, h = c // 2, c % 2
        out[b, h * TOK:(h + 1) * TOK, :] = res.results[c]["out"]
    return out
